# version 12
# baseline (speedup 1.0000x reference)
import contextlib
import numpy as np
import ml_dtypes
import concourse.bass as bass
import concourse.mybir as mybir
from concourse.bass_utils import run_bass_kernel_spmd
from concourse.bass_core_rust import IndirectOffsetOnAxis
from concourse.ap import AP

F32 = mybir.dt.float32
BF16 = mybir.dt.bfloat16
I8 = mybir.dt.int8
I32 = mybir.dt.int32
AF = mybir.ActivationFunctionType
ALU = mybir.AluOpType
BF = ml_dtypes.bfloat16
AX = mybir.AxisListType.X

B, S_, H, NH, KVH, D, II = 4, 2048, 4096, 32, 8, 128, 11008
NB, BLK, BT = 1024, 16, 512
TKV = BT * BLK
NCORE = 8
NTOK = 1024
NIT = II // 128
ROWS_KV = NB * 2 * KVH * BLK

ENGS = ("sync", "gpsimd", "tensor", "vector", "scalar")


class _Op:
    __slots__ = ("fn", "eng", "sem", "inc_amt", "waits", "needed", "count")

    def __init__(self, fn, eng, sem, inc_amt, waits):
        self.fn, self.eng, self.sem, self.inc_amt = fn, eng, sem, inc_amt
        self.waits = [w for w in waits if w is not None]
        self.needed = False
        self.count = None


class Sched:

    def __init__(self):
        self.ops = {e: [] for e in ENGS}
        self.sem_names = set()

    def op(self, eng, fn, waits=(), sem=None, dma=False, force_inc=False):
        sem = sem or eng
        self.sem_names.add(sem)
        o = _Op(fn, eng, sem, 16 if dma else 1, waits)
        o.needed = force_inc or dma
        self.ops[eng].append(o)
        return o

    def finalize(self):
        for e in ENGS:
            for o in self.ops[e]:
                if e != "vector":
                    o.waits = [w for w in o.waits if w.eng != e]
                for w in o.waits:
                    w.needed = True
        counts = {s: 0 for s in self.sem_names}
        self.dma_sems = {}
        for e in ENGS:
            for o in self.ops[e]:
                if o.needed:
                    counts[o.sem] += o.inc_amt
                    o.count = counts[o.sem]
                if o.inc_amt == 16:
                    self.dma_sems[o.sem] = e
        self.final_counts = counts

    def emit(self, eng_name, eng, sems):
        satisfied = {}
        for o in self.ops[eng_name]:
            best = {}
            for w in o.waits:
                assert w.count is not None, "wait on op without inc"
                if w.count > best.get(w.sem, -1):
                    best[w.sem] = w.count
            for s in sorted(best):
                if best[s] > satisfied.get(s, -1):
                    eng.wait_ge(sems[s], best[s])
                    satisfied[s] = best[s]
            ins = o.fn(eng)
            if o.needed:
                ins.then_inc(sems[o.sem], o.inc_amt)
        for s, e in self.dma_sems.items():
            if e == eng_name and self.final_counts[s] > 0:
                eng.wait_ge(sems[s], self.final_counts[s])


def build_program(taps=None, stage=6, nphase=2):
    taps = taps or {}
    nc = bass.Bass("TRN2", target_bir_lowering=False, debug=False,
                   detect_race_conditions=False)

    dx = nc.dram_tensor("x", [NTOK, H], F32, kind="ExternalInput").ap()
    dcos = nc.dram_tensor("cos", [128, NTOK], F32, kind="ExternalInput").ap()
    dsin = nc.dram_tensor("sin", [128, NTOK], F32, kind="ExternalInput").ap()
    dwq = nc.dram_tensor("wq_t", [8, 128, 32, 512], F32, kind="ExternalInput").ap()
    dwo = nc.dram_tensor("wo_t", [8, 128, 32, 512], F32, kind="ExternalInput").ap()
    dgate = nc.dram_tensor("gate_t", [NIT, 128, 32, 128], I8, kind="ExternalInput").ap()
    dup = nc.dram_tensor("up_t", [NIT, 128, 32, 128], I8, kind="ExternalInput").ap()
    ddown = nc.dram_tensor("down_t", [32, 128, NIT, 128], I8, kind="ExternalInput").ap()
    dgs = nc.dram_tensor("gs", [128, NIT], F32, kind="ExternalInput").ap()
    dus = nc.dram_tensor("us", [128, NIT], F32, kind="ExternalInput").ap()
    dds = nc.dram_tensor("ds", [128, 32], F32, kind="ExternalInput").ap()
    dkv = nc.dram_tensor("kv", [ROWS_KV, 128], F32, kind="ExternalInput").ap()
    dbt = nc.dram_tensor("bt_rep", [128, 4], I32, kind="ExternalInput").ap()
    djv = nc.dram_tensor("jvec", [128, 1], F32, kind="ExternalInput").ap()
    dident = nc.dram_tensor("ident", [128, 128], BF16, kind="ExternalInput").ap()
    dmt = nc.dram_tensor("mt_rope", [128, 128], BF16, kind="ExternalInput").ap()
    dones = nc.dram_tensor("ones", [128, 128], BF16, kind="ExternalInput").ap()
    dout = nc.dram_tensor("out", [NTOK, H], F32, kind="ExternalOutput").ap()
    dtaps = {k: nc.dram_tensor("tap_" + k, shp, dt, kind="ExternalOutput").ap()
             for k, (shp, dt) in taps.items()}

    def sbt(name, offset, shape, dtype):
        return nc.alloc_sbuf_tensor_at(name, shape, dtype, offset=offset + 128)

    cos_ph = sbt("cos_ph", 16384, [128, 512], F32)
    sin_ph = sbt("sin_ph", 18432, [128, 512], F32)
    ident = sbt("ident", 20480, [128, 128], BF16)
    mtr = sbt("mtr", 20736, [128, 128], BF16)
    ones = sbt("ones", 20992, [128, 128], BF16)
    cstf = sbt("cstf", 21504, [128, 16], F32)
    gs = sbt("gs", 21632, [128, NIT], F32)
    us = sbt("us", 21984, [128, NIT], F32)
    gu = sbt("gu", 22336, [128, NIT], F32)
    ds = sbt("ds", 22688, [128, 32], F32)
    bt_rep = sbt("bt_rep", 22816, [128, 4], I32)
    jvec = sbt("jvec", 23072, [128, 1], F32)
    idxk2 = sbt("idxk2", 23360, [128, 8], I32)
    idxv2 = sbt("idxv2", 23872, [128, 8], I32)
    lt = sbt("lt", 24384, [128, 16], F32)
    lt2 = sbt("lt2", 24448, [128, 4], F32)
    dacc = sbt("dacc", 24480, [128, 2 * 512], F32)
    qrope = sbt("qrope", 28576, [128, 4 * 512], BF16)
    qb = sbt("qb", 32672, [128, 512], BF16)
    rt1 = sbt("rt1", 33696, [128, 512], F32)
    rt2 = sbt("rt2", 35744, [128, 512], F32)
    recip = sbt("recip", 37792, [128, 512], F32)
    utmp = sbt("utmp", 24480, [128, 2 * 512], F32)
    stmp = sbt("stmp", 28576, [128, 2 * 512], F32)
    pbuf = sbt("pbuf", 32672, [128, 2 * 512], BF16)

    xnT = sbt("xnT", 39840, [128, 32 * 512], BF16)
    dslabA = sbt("dslabA", 39840, [128, NIT * 128], BF16)
    ctx = sbt("ctx", 72608, [128, 32 * 512], BF16)
    x_tile = sbt("x_tile", 72608, [128, 4096], F32)
    sqscr = sbt("sqscr", 88992, [128, 4096], F32)
    xnbuf = sbt("xnbuf", 88992, [128, 4096], BF16)
    gslab = sbt("gslab", 72608, [128, 2 * 32 * 128], BF16)
    uslab = sbt("uslab", 88992, [128, 2 * 32 * 128], BF16)
    dslabB = sbt("dslabB", 72608, [128, NIT * 128], BF16)
    xmt = sbt("xmt", 94624, [128, 2 * 512], F32)
    ostg = sbt("ostg", 98720, [128, 2 * 512], F32)
    wqslab = sbt("wqslab", 105376, [128, 32 * 512], BF16)
    s_h = sbt("s_h", 105376, [128, 32 * 512], BF16)
    KT = [sbt("KT0", 138144, [128, TKV], BF16), sbt("KT1", 170912, [128, TKV], BF16)]
    VH = [sbt("V0", 154528, [128, TKV], BF16), sbt("V1", 187296, [128, TKV], BF16)]
    kst = sbt("kst", 203680, [128, 32 * 128], BF16)
    expr = sbt("expr", 211872, [128, 8 * 512], BF16)
    wo_sl = [sbt("woA", 138144, [128, 32 * 512], BF16),
             sbt("woB", 170912, [128, 32 * 512], BF16)]
    h_half = sbt("h_half", 138144, [128, NIT * 512], BF16)
    idxf = sbt("idxf", 226208, [128, 4], F32)
    idxf2 = sbt("idxf2", 226464, [128, 4], F32)
    idxf3 = sbt("idxf3", 226720, [128, 4], F32)
    datmp = sbt("datmp", 220064, [128, 3 * 512], BF16)
    dacb = sbt("dacb", 223136, [128, 512], BF16)
    lnscr = sbt("lnscr", 224160, [128, 512], F32)

    pb = [nc.alloc_psum_tensor(f"pb{i}", [128, 512], F32) for i in range(2)]
    SS = [nc.alloc_psum_tensor(f"ps{i}", [128, 1024], F32) for i in range(2)]
    pq6 = nc.alloc_psum_tensor("pq6", [128, 512], F32)
    pr7 = nc.alloc_psum_tensor("pr7", [128, 512], F32)
    nc.sbuf_base = 227104

    P = Sched()
    bank_guard = {}

    def bw(k):
        return tuple(bank_guard.get(k, ()))

    def bs(k, *toks):
        bank_guard[k] = [t for t in toks if t is not None]

    def OP(eng, fn, waits=(), sem=None, dma=False, force_inc=False):
        return P.op(eng, fn, waits, sem=sem, dma=dma, force_inc=force_inc)

    def vop(fn, waits=()):
        return OP("vector", fn, waits)

    ld = []
    for dst, src in ((ident, dident), (mtr, dmt), (ones, dones), (gs, dgs),
                     (us, dus), (ds, dds), (bt_rep, dbt), (jvec, djv)):
        ld.append(OP("sync", lambda e, d=dst, s=src: e.dma_start(d[:, :], s),
                     sem="dS", dma=True))
    cl = ld[-1]
    CF = {"inv4096": 0, "eps": 1, "expbias": 2, "rsqrtd": 3, "neg1": 4, "negh": 15}
    m0 = None
    for k, v in (("inv4096", 1.0 / 4096), ("eps", 1e-5), ("expbias", -8.0),
                 ("rsqrtd", float(1.0 / np.sqrt(128.0))), ("neg1", -1.0),
                 ("negh", -0.5)):
        m0 = vop(lambda e, c=CF[k], v=v: e.memset(cstf[:, c:c + 1], v))
    for g in range(8):
        m0 = vop(lambda e, g=g: e.memset(cstf[:, 7 + g:8 + g], float(g * 16)))
    cvt = vop(lambda e: e.tensor_copy(idxf[:, :], bt_rep[:, :]), waits=(cl, m0))
    c256 = vop(lambda e: e.memset(cstf[:, 5:6], 256.0))
    c128 = vop(lambda e: e.memset(cstf[:, 6:7], 128.0))
    idxb = vop(lambda e: e.tensor_single_scalar(idxf2[:, :], idxf[:, :],
                                                cstf[:, 5:6], ALU.mult),
               waits=(cvt, c256, c128))
    gu_t = vop(lambda e: e.tensor_mul(gu[:, :], gs[:, :], us[:, :]), waits=(cl,))
    consts_ready = vop(lambda e: e.tensor_copy(lt2[:, 2:3], cstf[:, 0:1]),
                       waits=(idxb, gu_t))

    idx_guard = {}

    def make_idx(g, waits):
        b = (g % 2) * 4
        iw = idx_guard.get(g % 2)
        t0 = vop(lambda e, g=g: e.tensor_single_scalar(
            idxf3[:, :], idxf2[:, :], cstf[:, 7 + g:8 + g], ALU.add),
            waits=tuple(waits) + (consts_ready, iw))
        t1 = vop(lambda e, b=b: e.tensor_copy(idxk2[:, b:b + 4], idxf3[:, :]),
                 waits=(t0,))
        tv = vop(lambda e: e.tensor_single_scalar(
            idxf[:, :], idxf3[:, :], cstf[:, 6:7], ALU.add), waits=(t0,))
        t2 = vop(lambda e, b=b: e.tensor_copy(idxv2[:, b:b + 4], idxf[:, :]),
                 waits=(tv, t1))
        return t2

    kst_guard = {}
    gather_done = {}

    def plan_gathers(g, waits):
        b2 = g % 2
        ib = b2 * 4
        last_tr = None
        last_v = None
        for q in range(4):
            slot = q % 2
            war = kst_guard.get(slot)
            gk = OP("gpsimd", lambda e, q=q, ib=ib, slot=slot: e.indirect_dma_start(
                kst[:, slot * 2048:(slot + 1) * 2048], None, dkv,
                IndirectOffsetOnAxis(idxk2[:, ib + q:ib + q + 1], 0)),
                waits=tuple(waits) + ((war,) if war else ()), sem="dG", dma=True)
            for j in range(16):
                c = q * 16 + j
                last_tr = OP("sync", lambda e, c=c, j=j, b2=b2, slot=slot:
                             e.dma_start_transpose(
                    KT[b2][:, c * 128:(c + 1) * 128],
                    kst[:, slot * 2048 + j * 128: slot * 2048 + (j + 1) * 128]),
                    waits=(gk,), sem="dT", dma=True)
            kst_guard[slot] = last_tr
        for q in range(4):
            last_v = OP("gpsimd", lambda e, q=q, b2=b2, ib=ib: e.indirect_dma_start(
                VH[b2][:, q * 2048:(q + 1) * 2048], None, dkv,
                IndirectOffsetOnAxis(idxv2[:, ib + q:ib + q + 1], 0)),
                waits=tuple(waits), sem="dG", dma=True)
        idx_guard[b2] = last_v
        gather_done[g] = (last_tr, last_v)

    prev_phase_done = (consts_ready,)
    out_tokens = []

    for ph in range(nphase):
        pw = tuple(prev_phase_done)
        ck = OP("sync", lambda e, ph=ph: e.dma_start(
            cos_ph[:, :], dcos[:, ph * 512:(ph + 1) * 512]), waits=pw,
            sem="dS", dma=True)
        sk = OP("sync", lambda e, ph=ph: e.dma_start(
            sin_ph[:, :], dsin[:, ph * 512:(ph + 1) * 512]), waits=pw,
            sem="dS", dma=True)

        it0 = make_idx(0, pw)
        plan_gathers(0, pw + (it0,))
        wq_tok = OP("gpsimd", lambda e: e.dma_start(wqslab[:, :], dwq[0]),
                    waits=pw, sem="dG", dma=True)

        xn_evac = {}
        last_xn = None
        last_rs = None
        for rt in range(4):
            xl = OP("sync", lambda e, rt=rt, ph=ph: e.dma_start(
                x_tile[:, :], dx[ph * 512 + rt * 128: ph * 512 + (rt + 1) * 128, :]),
                waits=pw + ((last_xn, last_rs) if last_xn else ()),
                sem="dS", dma=True)
            sqw = (xn_evac[(rt - 1, 3)],) if rt > 0 else ()
            rs = vop(lambda e: e.reduce_sum(lt[:, 0:1], x_tile[:, :], AX),
                     waits=(xl,))
            last_rs = rs
            sq = OP("scalar", lambda e: e.activation(
                sqscr[:, :], x_tile[:, :], AF.Square, accum_out=lt[:, 1:2]),
                waits=(xl, consts_ready) + sqw)
            s1 = vop(lambda e: e.tensor_single_scalar(lt[:, 2:3], lt[:, 0:1],
                                                      cstf[:, 0:1], ALU.mult),
                     waits=(rs, sq))
            s2 = vop(lambda e: e.tensor_single_scalar(lt[:, 3:4], lt[:, 1:2],
                                                      cstf[:, 0:1], ALU.mult),
                     waits=(s1,))
            s3 = vop(lambda e: e.tensor_mul(lt[:, 4:5], lt[:, 2:3], lt[:, 2:3]),
                     waits=(s1,))
            s4 = vop(lambda e: e.tensor_sub(lt[:, 5:6], lt[:, 3:4], lt[:, 4:5]),
                     waits=(s2, s3))
            s5 = vop(lambda e: e.tensor_single_scalar(lt[:, 6:7], lt[:, 5:6],
                                                      cstf[:, 1:2], ALU.add),
                     waits=(s4,))
            s6 = vop(lambda e: e.tensor_copy(lt[:, 7:8], lt[:, 6:7]),
                     waits=(s5,))
            s7a = OP("scalar", lambda e: e.activation(lt[:, 9:10], lt[:, 7:8],
                                                      AF.Ln), waits=(s6,))
            s7 = OP("scalar", lambda e: e.activation(lt2[:, 0:1], lt[:, 9:10],
                                                     AF.Exp, scale=cstf[:, 15:16]),
                    waits=(s7a,))
            s8 = vop(lambda e: e.tensor_mul(lt[:, 8:9], lt[:, 2:3], lt2[:, 0:1]),
                     waits=(s7, s1))
            s9 = vop(lambda e: e.tensor_single_scalar(lt2[:, 1:2], lt[:, 8:9],
                                                      cstf[:, 4:5], ALU.mult),
                     waits=(s8,))
            xn = OP("scalar", lambda e: e.activation(
                xnbuf[:, :], x_tile[:, :], AF.Identity,
                bias=lt2[:, 1:2], scale=lt2[:, 0:1]), waits=(s9,))
            last_xn = xn
            for fb in range(4):
                sj = fb % 2
                tt = None
                for j in range(8):
                    ft = fb * 8 + j
                    tt = OP("tensor", lambda e, ft=ft, j=j, sj=sj: e.transpose(
                        SS[sj].ap().bitcast(BF16)[:, j * 128:(j + 1) * 128],
                        xnbuf[:, ft * 128:(ft + 1) * 128], ident[:, :]),
                        waits=(xn, cl) + bw("S%d" % sj))
                ev = vop(lambda e, fb=fb, rt=rt, sj=sj: e.tensor_copy(
                    AP(xnT.ap().tensor, fb * 8 * 512 + rt * 128,
                       [[32 * 512, 128], [512, 8], [1, 128]]),
                    AP(SS[sj].ap().bitcast(BF16).tensor, 0,
                       [[2048, 128], [128, 8], [1, 128]])),
                    waits=(tt,))
                bs("S%d" % sj, ev)
                xn_evac[(rt, fb)] = ev
        xnT_ready = tuple(xn_evac.values())
        if "xnT" in taps and ph == 0:
            OP("sync", lambda e: e.dma_start(dtaps["xnT"], xnT[:, :]),
               waits=xnT_ready, sem="dS", dma=True)

        if stage < 2:
            prev_phase_done = (last_xn,)
            continue
        ctx_ev = {}
        sweep_end = {}
        rope_tok = {}
        qp_tok = {}
        sm_last = {}
        dacc_guard = {}
        dacb_guard = [None]
        ring_guard = {}
        wq_cur = {0: wq_tok}

        def qproj_rope(gg, qh):
            mm = None
            for k in range(32):
                mm = OP("tensor", lambda e, k=k, qh=qh: e.matmul(
                    pq6[:, :],
                    wqslab[:, k * 512 + qh * 128: k * 512 + (qh + 1) * 128],
                    xnT[:, k * 512:(k + 1) * 512],
                    start=(k == 0), stop=(k == 31)),
                    waits=(wq_cur[gg],) + xnT_ready + bw("b6"))
            finish_qproj(gg, qh, mm)()

        def finish_qproj(gg, qh, mm):
            qp_tok[(gg, qh)] = mm
            qbc = vop(lambda e: e.tensor_copy(qb[:, :], pq6[:, :]), waits=(mm,))

            def rot_rest():
                rot = OP("tensor", lambda e: e.matmul(
                    pr7[:, :], mtr[:, :], qb[:, :], start=True, stop=True),
                    waits=(qbc, cl) + bw("b7"))
                v1 = vop(lambda e: e.tensor_mul(rt1[:, :], pq6[:, :], cos_ph[:, :]),
                         waits=(mm, sk, ck, qbc))
                bs("b6", qbc, v1)
                v2 = vop(lambda e: e.tensor_mul(rt2[:, :], pr7[:, :], sin_ph[:, :]),
                         waits=(rot,))
                bs("b7", v2)
                slotw = sm_last.get((gg - 1, qh))
                v3 = vop(lambda e, qh=qh: e.tensor_add(
                    qrope[:, qh * 512:(qh + 1) * 512], rt1[:, :], rt2[:, :]),
                    waits=(v1, v2) + ((slotw,) if slotw else ()))
                rope_tok[(gg, qh)] = v3
            return rot_rest

        filler = []
        tail_q = []
        rotq = []

        def queue_qproj(gg, qh):
            st = {"mm": None}
            for k in range(32):
                def thunk(k=k, gg=gg, qh=qh, st=st):
                    st["mm"] = OP("tensor", lambda e, k=k, qh=qh: e.matmul(
                        pq6[:, :],
                        wqslab[:, k * 512 + qh * 128: k * 512 + (qh + 1) * 128],
                        xnT[:, k * 512:(k + 1) * 512],
                        start=(k == 0), stop=(k == 31)),
                        waits=(wq_cur[gg],) + xnT_ready + bw("b6"))
                    if k == 31:
                        rotq.append(finish_qproj(gg, qh, st["mm"]))
                filler.append(thunk)

        for qh in range(3):
            qproj_rope(0, qh)
        if ph == 0 and "qrope" in taps:
            OP("sync", lambda e: e.dma_start(dtaps["qrope"], qrope[:, 0:3 * 512]),
               waits=(rope_tok[(0, 2)],), sem="dS", dma=True)

        ngroups = 8 if stage >= 3 else 1
        for g in range(ngroups):
            b2 = g % 2
            ktr_last, vg_last = gather_done[g]
            if g == 0 and ph == 0:
                if "kt0" in taps:
                    OP("sync", lambda e: e.dma_start(dtaps["kt0"], KT[0][:, :]),
                       waits=(ktr_last,), sem="dS", dma=True)
                if "v0" in taps:
                    OP("sync", lambda e: e.dma_start(dtaps["v0"], VH[0][:, :]),
                       waits=(vg_last,), sem="dS", dma=True)

            for qh in range(4):
                if qh == 1 and g < ngroups - 1:
                    bfr = sweep_end.get(g - 1)
                    wq_cur[g + 1] = OP("gpsimd", lambda e, g=g: e.dma_start(
                        wqslab[:, :], dwq[g + 1]), waits=(qp_tok[(g, 3)],),
                        sem="dG", dma=True)
                    itk = make_idx(g + 1, ())
                    plan_gathers(g + 1, (itk,) + ((bfr,) if bfr else ()))
                tgt = g * 4 + qh + 3
                if tgt < ngroups * 4:
                    queue_qproj(tgt // 4, tgt % 4)
                ctxk = "b%d" % (qh % 2)
                ctxp = pb[qh % 2]
                dac = dacc[:, (qh % 2) * 512:(qh % 2 + 1) * 512]
                da = None
                ex_tok = {}
                cx_tok = {}
                defer_cx = []
                cx_l = None
                sm_l = None
                for c2 in range(34):
                  if c2 < 32:
                    j = c2 % 2
                    sk_ = "S%d" % j
                    sms = []
                    for hh in range(2):
                        c = 2 * c2 + hh
                        sm = OP("tensor", lambda e, c=c, qh=qh, j=j, hh=hh, b2=b2: e.matmul(
                            SS[j][:, hh * 512:(hh + 1) * 512],
                            KT[b2][:, c * 128:(c + 1) * 128],
                            qrope[:, qh * 512:(qh + 1) * 512],
                            start=True, stop=True),
                            waits=(rope_tok[(g, qh)], ktr_last) + bw(sk_))
                        sms.append(sm)
                    sm_l = sms[-1]
                    slot = c2 % 4
                    rw = ring_guard.get(slot, ())
                    ex = OP("scalar", lambda e, c2=c2, j=j, slot=slot: e.activation(
                        expr[:, slot * 1024:(slot + 1) * 1024], SS[j][:, :],
                        AF.Exp, bias=cstf[:, 2:3], scale=cstf[:, 3:4]),
                        waits=(sms[1], consts_ready) + tuple(rw))
                    ex_tok[c2] = ex
                    bs(sk_, ex)
                  if c2 >= 2:
                    p2 = c2 - 2
                    slotp = p2 % 4
                    exp_ = ex_tok[p2]
                    cxs = []
                    for hh in range(2):
                        c = 2 * p2 + hh
                        cxw = bw(ctxk) if c == 0 else ()
                        cx = OP("tensor", lambda e, c=c, hh=hh, slot=slotp, b2=b2,
                                ctxp=ctxp: e.matmul(
                            ctxp[:, :], VH[b2][:, c * 128:(c + 1) * 128],
                            expr[:, slot * 1024 + hh * 512: slot * 1024 + (hh + 1) * 512],
                            start=(c == 0), stop=(c == 63)),
                            waits=(exp_, vg_last) + cxw)
                        cxs.append(cx)
                    cx_l = cxs[-1]
                    cx_tok[p2] = cx_l
                    for (sslot, cc2) in defer_cx:
                        if cc2 == p2:
                            ring_guard[sslot].append(cx_l)
                    defer_cx = [d for d in defer_cx if d[1] != p2]
                  if c2 < 32 and c2 % 2 == 1:
                        sa = (c2 - 1) % 4
                        sb = c2 % 4
                        q1 = vop(lambda e, sa=sa: e.tensor_add(
                            datmp[:, 0:512], expr[:, sa * 1024:sa * 1024 + 512],
                            expr[:, sa * 1024 + 512:(sa + 1) * 1024]),
                            waits=(ex_tok[c2 - 1],))
                        q2 = vop(lambda e, sb=sb: e.tensor_add(
                            datmp[:, 512:1024], expr[:, sb * 1024:sb * 1024 + 512],
                            expr[:, sb * 1024 + 512:(sb + 1) * 1024]),
                            waits=(ex,))
                        q3 = vop(lambda e: e.tensor_add(
                            datmp[:, 1024:1536], datmp[:, 0:512],
                            datmp[:, 512:1024]), waits=(q1, q2))
                        if c2 == 1:
                            dgw = dacc_guard.get(qh % 2)
                            da = vop(lambda e, dac=dac: e.tensor_copy(
                                dac, datmp[:, 1024:1536]),
                                waits=(q3,) + ((dgw,) if dgw else ()))
                        else:
                            da = vop(lambda e, dac=dac: e.tensor_add(
                                dac, dac, datmp[:, 1024:1536]), waits=(q3, da))
                        ring_guard[sa] = [q1]
                        defer_cx.append((sa, c2 - 1))
                        ring_guard[sb] = [q2]
                        defer_cx.append((sb, c2))
                  if rotq:
                    rotq.pop(0)()
                  if tail_q and c2 >= 8:
                    tail_q.pop(0)()
                  if filler and c2 >= 16:
                    filler.pop(0)()
                    if filler:
                        filler.pop(0)()
                sm_last[(g, qh)] = sm_l

                def denom_tail(g=g, qh=qh, ctxp=ctxp, dac=dac, da=da, cx_l=cx_l):
                    dbw = dacb_guard[0]
                    dct = vop(lambda e, dac=dac: e.tensor_copy(dacb[:, :], dac),
                              waits=(da,) + ((dbw,) if dbw else ()))
                    dacc_guard[qh % 2] = dct
                    rd = OP("tensor", lambda e: e.matmul(
                        pr7[:, :], ones[:, :], dacb[:, :], start=True, stop=True),
                        waits=(dct, cl) + bw("b7"))
                    dacb_guard[0] = rd
                    rca = OP("scalar", lambda e: e.activation(
                        lnscr[:, :], pr7[:, :], AF.Ln), waits=(rd,))
                    rc = OP("scalar", lambda e: e.activation(
                        recip[:, :], lnscr[:, :], AF.Exp, scale=cstf[:, 4:5]),
                        waits=(rca,))
                    bs("b7", rca)
                    cev = vop(lambda e, g=g, qh=qh, ctxp=ctxp: e.tensor_mul(
                        ctx[:, (g * 4 + qh) * 512:(g * 4 + qh + 1) * 512],
                        ctxp[:, :], recip[:, :]), waits=(rc, cx_l))
                    ctx_ev[(g, qh)] = cev
                    bs("b%d" % (qh % 2), cev)
                tail_q.append(denom_tail)
                sweep_end[g] = cx_l
                while rotq:
                    rotq.pop(0)()
                if qh == 3 and g == ngroups - 1:
                    while tail_q:
                        tail_q.pop(0)()
                    while filler:
                        filler.pop(0)()

        ctx_all = tuple(ctx_ev[(g, qh)] for g in range(ngroups) for qh in range(4))
        if stage < 3:
            prev_phase_done = ctx_all
            continue
        if "ctx" in taps and ph == 0:
            OP("sync", lambda e: e.dma_start(dtaps["ctx"], ctx[:, :]),
               waits=ctx_all, sem="dS", dma=True)

        wo_tok = [None, None]
        wo_tok[0] = OP("gpsimd", lambda e: e.dma_start(wo_sl[0][:, :], dwo[0]),
                       waits=(sweep_end[6],), sem="dG", dma=True)
        wo_tok[1] = OP("gpsimd", lambda e: e.dma_start(wo_sl[1][:, :], dwo[1]),
                       waits=(sweep_end[7],), sem="dG", dma=True)
        s_ev = {}
        womm = {}
        for og in range(8):
            if og >= 2:
                wo_tok[og % 2] = OP("gpsimd", lambda e, og=og: e.dma_start(
                    wo_sl[og % 2][:, :], dwo[og]),
                    waits=(womm[(og - 2, 3)],), sem="dG", dma=True)
            for m in range(4):
                sj = m % 2
                bk = "S%d" % sj
                mm = None
                for k in range(32):
                    mm = OP("tensor", lambda e, k=k, m=m, og=og, sj=sj: e.matmul(
                        SS[sj][:, 0:512],
                        wo_sl[og % 2][:, k * 512 + m * 128: k * 512 + (m + 1) * 128],
                        ctx[:, k * 512:(k + 1) * 512],
                        start=(k == 0), stop=(k == 31)),
                        waits=(wo_tok[og % 2],) + ctx_all + bw(bk))
                womm[(og, m)] = mm
                ev = OP("scalar", lambda e, og=og, m=m, sj=sj: e.copy(
                    s_h[:, (og * 4 + m) * 512:(og * 4 + m + 1) * 512],
                    SS[sj][:, 0:512]), waits=(mm,))
                s_ev[(og, m)] = ev
                bs(bk, ev)
        if "s" in taps and ph == 0:
            OP("sync", lambda e: e.dma_start(dtaps["s"], s_h[:, :]),
               waits=tuple(s_ev.values()), sem="dS", dma=True)
        s_last = s_ev[(7, 3)]

        gsl_tok = {}
        hev = {}
        gmm = {}
        for it in range(86):
            sl = it % 2
            gss = gslab[:, sl * 4096:(sl + 1) * 4096]
            uss = uslab[:, sl * 4096:(sl + 1) * 4096]
            if it < 2:
                ldw = (s_last, womm[(7, 3)])
            else:
                ldw = (gmm[it - 2],)
            gt = OP("gpsimd", lambda e, it=it, gss=gss: e.dma_start(gss, dgate[it]),
                    waits=ldw, sem="dG", dma=True)
            ut = OP("gpsimd", lambda e, it=it, uss=uss: e.dma_start(uss, dup[it]),
                    waits=ldw, sem="dG", dma=True)
            gsl_tok[it] = (gt, ut)
            gk_ = "b%d" % sl
            up = pq6 if sl == 0 else pr7
            uk = "b6" if sl == 0 else "b7"
            mmg = mmu = None
            for k in range(32):
                mmg = OP("tensor", lambda e, k=k, sl=sl, gss=gss: e.matmul(
                    pb[sl][:, :], gss[:, k * 128:(k + 1) * 128],
                    xnT[:, k * 512:(k + 1) * 512],
                    start=(k == 0), stop=(k == 31)),
                    waits=(gt,) + xnT_ready + bw(gk_))
            for k in range(32):
                mmu = OP("tensor", lambda e, k=k, up=up, uss=uss: e.matmul(
                    up[:, :], uss[:, k * 128:(k + 1) * 128],
                    xnT[:, k * 512:(k + 1) * 512],
                    start=(k == 0), stop=(k == 31)),
                    waits=(ut,) + bw(uk))
            gmm[it] = mmu
            uev = OP("scalar", lambda e, sl=sl, up=up: e.copy(
                utmp[:, sl * 512:(sl + 1) * 512], up[:, :]), waits=(mmu,))
            he = vop(lambda e, it=it, sl=sl: e.scalar_tensor_tensor(
                h_half[:, it * 512:(it + 1) * 512], pb[sl][:, :], gu[:, it:it + 1],
                utmp[:, sl * 512:(sl + 1) * 512], ALU.mult, ALU.mult),
                waits=(uev, mmg, consts_ready))
            hev[it] = he
            bs(gk_, he)
            bs(uk, uev, he)
        h_all = tuple(hev.values())
        if "h" in taps and ph == 0:
            OP("sync", lambda e: e.dma_start(dtaps["h"], h_half[:, 0:8 * 512]),
               waits=h_all, sem="dS", dma=True)

        dsl = [dslabA, dslabB]
        dtok = [None, None]
        comb = {}
        dmm = {}
        fin_q = []
        out_box = [None]
        for mt in range(32):
            sl = mt % 2
            if mt < 2:
                ldw = (gmm[85], hev[85], gsl_tok[85][sl])
            else:
                ldw = (dmm[mt - 2],)
            dtok[sl] = OP("gpsimd", lambda e, mt=mt, sl=sl: e.dma_start(
                dsl[sl][:, :], ddown[mt]), waits=ldw, sem="dG", dma=True)
            xw = (comb[(mt - 2, 3)],) if mt >= 2 else \
                (womm[(7, 3)], gmm[85], gsl_tok[85][0], gsl_tok[85][1])
            xts = []
            for tt in range(4):
                xts.append(OP("sync", lambda e, mt=mt, tt=tt, sl=sl, ph=ph: e.dma_start(
                    xmt[:, sl * 512 + tt * 128: sl * 512 + (tt + 1) * 128],
                    dx[ph * 512 + tt * 128: ph * 512 + (tt + 1) * 128,
                       mt * 128:(mt + 1) * 128]),
                    waits=xw, sem="dS", dma=True))
            sj = sl
            bk = "S%d" % sj
            mm = None
            for ki in range(86):
                mm = OP("tensor", lambda e, ki=ki, sj=sj, sl=sl: e.matmul(
                    SS[sj][:, 0:512], dsl[sl][:, ki * 128:(ki + 1) * 128],
                    h_half[:, ki * 512:(ki + 1) * 512],
                    start=(ki == 0), stop=(ki == 85)),
                    waits=(dtok[sl],) + h_all + bw(bk))
                if ki == 12 and fin_q:
                    fin_q.pop(0)()
            dmm[mt] = mm
            si = OP("scalar", lambda e, mt=mt, sl=sl, sj=sj: e.activation(
                stmp[:, sl * 512:(sl + 1) * 512], SS[sj][:, 0:512], AF.Silu,
                scale=ds[:, mt:mt + 1]), waits=(mm, consts_ready))
            bs(bk, si)
            pe = vop(lambda e, mt=mt, sl=sl: e.tensor_mul(
                pbuf[:, sl * 512:(sl + 1) * 512],
                stmp[:, sl * 512:(sl + 1) * 512],
                s_h[:, mt * 512:(mt + 1) * 512]),
                waits=(si, s_ev[(mt // 4, mt % 4)]))

            def fin(mt=mt, sl=sl, pe=pe, xts=xts, ph=ph):
                tbp = pq6 if sl == 0 else pr7
                tbk = "b6" if sl == 0 else "b7"
                trs = []
                for tt in range(4):
                    trs.append(OP("tensor", lambda e, tt=tt, sl=sl, tbp=tbp: e.transpose(
                        tbp.ap().bitcast(BF16)[:, tt * 128:(tt + 1) * 128],
                        pbuf[:, sl * 512 + tt * 128: sl * 512 + (tt + 1) * 128],
                        ident[:, :]),
                        waits=(pe, cl) + bw(tbk)))
                cas = []
                for tt in range(4):
                    cadd = vop(lambda e, tt=tt, sl=sl, tbp=tbp: e.tensor_add(
                        ostg[:, sl * 512 + tt * 128: sl * 512 + (tt + 1) * 128],
                        tbp.ap().bitcast(BF16)[:, tt * 128:(tt + 1) * 128],
                        xmt[:, sl * 512 + tt * 128: sl * 512 + (tt + 1) * 128]),
                        waits=(trs[3], xts[tt]))
                    comb[(mt, tt)] = cadd
                    cas.append(cadd)
                    out_box[0] = OP("sync", lambda e, mt=mt, tt=tt, sl=sl, ph=ph: e.dma_start(
                        dout[ph * 512 + tt * 128: ph * 512 + (tt + 1) * 128,
                             mt * 128:(mt + 1) * 128],
                        ostg[:, sl * 512 + tt * 128: sl * 512 + (tt + 1) * 128]),
                        waits=(cadd,), sem="dO", dma=True, force_inc=True)
                    out_tokens.append(out_box[0])
                bs(tbk, *cas)
            fin_q.append(fin)
        while fin_q:
            fin_q.pop(0)()

        prev_phase_done = (out_box[0], dmm[31], comb[(31, 3)], comb[(30, 3)])

    if out_tokens:
        OP("sync", lambda e: e.dma_start(lt[:, 0:16], dx[0:128, 0:16]),
           waits=(out_tokens[-1],), sem="dS", dma=True)

    P.finalize()
    stack = contextlib.ExitStack()
    sems = {s: stack.enter_context(nc.semaphore(f"sem_{s}"))
            for s in sorted(P.sem_names)}
    with nc.Block() as blk:
        blk.sync(lambda e: P.emit("sync", e, sems))
        blk.gpsimd(lambda e: P.emit("gpsimd", e, sems))
        blk.tensor(lambda e: P.emit("tensor", e, sems))
        blk.vector(lambda e: P.emit("vector", e, sems))
        blk.scalar(lambda e: P.emit("scalar", e, sems))
    stack.close()
    return nc



def _byteslice_i8(w):
    w = np.ascontiguousarray(np.asarray(w).astype("<i4", copy=False))
    return w.view(np.int8).reshape(w.shape + (4,))[..., 0]


def _prep(inputs):
    hs = np.ascontiguousarray(np.asarray(inputs["L0_attn_hidden_states"], dtype=np.float32))
    Wq = np.ascontiguousarray(np.asarray(inputs["L0_attn_Wq"], dtype=np.float32))
    Wo = np.ascontiguousarray(np.asarray(inputs["L0_attn_Wo"], dtype=np.float32))
    kv = np.ascontiguousarray(np.asarray(inputs["L0_attn_kv_cache"], dtype=np.float32))
    bt = np.asarray(inputs["L0_attn_block_table"], dtype=np.int32)
    g8 = _byteslice_i8(inputs["L0_ffn_gate_wq"])
    u8 = _byteslice_i8(inputs["L0_ffn_up_wq"])
    d8 = _byteslice_i8(inputs["L0_ffn_down_wq"])
    gsc = np.asarray(inputs["L0_ffn_gate_scale"], dtype=np.float32)
    usc = np.asarray(inputs["L0_ffn_up_scale"], dtype=np.float32)
    dsc = np.asarray(inputs["L0_ffn_down_scale"], dtype=np.float32)

    wq_t = np.ascontiguousarray(Wq.reshape(32, 128, 8, 512).transpose(2, 1, 0, 3))
    wo_t = np.ascontiguousarray(Wo.reshape(32, 128, 8, 512).transpose(2, 1, 0, 3))
    gate_t = np.ascontiguousarray(
        np.ascontiguousarray(g8.T).reshape(32, 128, NIT, 128).transpose(2, 1, 0, 3))
    up_t = np.ascontiguousarray(
        np.ascontiguousarray(u8.T).reshape(32, 128, NIT, 128).transpose(2, 1, 0, 3))
    down_t = np.ascontiguousarray(
        np.ascontiguousarray(d8.T).reshape(NIT, 128, 32, 128).transpose(2, 1, 0, 3))
    gs_t = np.ascontiguousarray(gsc.reshape(NIT, 128).T)
    us_t = np.ascontiguousarray(usc.reshape(NIT, 128).T)
    ds_t = np.ascontiguousarray(dsc.reshape(32, 128).T)
    kvf = kv.reshape(ROWS_KV, 128)
    bt_rep = np.ascontiguousarray(bt.reshape(4, 128).T).astype(np.int32)
    jv = (np.arange(128) % 16).astype(np.float32).reshape(128, 1)
    ident = np.eye(128, dtype=BF)
    mtr = np.zeros((128, 128), dtype=BF)
    for a in range(64):
        mtr[64 + a, a] = -1.0
        mtr[a, 64 + a] = 1.0
    ones = np.ones((128, 128), dtype=BF)

    invf = 1.0 / (10000.0 ** (np.arange(0, D, 2, dtype=np.float64) / D))
    in_maps = []
    xs = hs.reshape(NCORE, NTOK, H)
    for c in range(NCORE):
        pos = (c % 2) * NTOK + np.arange(NTOK, dtype=np.float64)
        ang = pos[:, None] * invf[None, :]
        emb = np.concatenate([ang, ang], axis=1)
        cos_t = np.ascontiguousarray(np.cos(emb).T.astype(np.float32))
        sin_t = np.ascontiguousarray(np.sin(emb).T.astype(np.float32))
        in_maps.append(dict(
            x=xs[c], cos=cos_t, sin=sin_t, wq_t=wq_t, wo_t=wo_t,
            gate_t=gate_t, up_t=up_t, down_t=down_t, gs=gs_t, us=us_t, ds=ds_t,
            kv=kvf, bt_rep=bt_rep, jvec=jv, ident=ident, mt_rope=mtr, ones=ones))
    return in_maps


def kernel(**inputs):
    in_maps = _prep(inputs)
    nc = build_program()
    res = run_bass_kernel_spmd(nc, in_maps, core_ids=list(range(NCORE)))
    out = np.concatenate([np.asarray(r["out"]) for r in res.results], axis=0)
    return np.ascontiguousarray(out.reshape(B, S_, H).astype(np.float32))


# version 17
# speedup vs baseline: 1.0155x; 1.0155x over previous
import contextlib
import numpy as np
import ml_dtypes
import concourse.bass as bass
import concourse.mybir as mybir
from concourse.bass_utils import run_bass_kernel_spmd
from concourse.bass_core_rust import IndirectOffsetOnAxis
from concourse.ap import AP

F32 = mybir.dt.float32
BF16 = mybir.dt.bfloat16
I8 = mybir.dt.int8
I32 = mybir.dt.int32
AF = mybir.ActivationFunctionType
ALU = mybir.AluOpType
BF = ml_dtypes.bfloat16
AX = mybir.AxisListType.X

B, S_, H, NH, KVH, D, II = 4, 2048, 4096, 32, 8, 128, 11008
NB, BLK, BT = 1024, 16, 512
TKV = BT * BLK
NCORE = 8
NTOK = 1024
NIT = II // 128
ROWS_KV = NB * 2 * KVH * BLK

ENGS = ("sync", "gpsimd", "tensor", "vector", "scalar")


class _Op:
    __slots__ = ("fn", "eng", "sem", "inc_amt", "waits", "needed", "count")

    def __init__(self, fn, eng, sem, inc_amt, waits):
        self.fn, self.eng, self.sem, self.inc_amt = fn, eng, sem, inc_amt
        self.waits = [w for w in waits if w is not None]
        self.needed = False
        self.count = None


class Sched:

    def __init__(self):
        self.ops = {e: [] for e in ENGS}
        self.sem_names = set()

    def op(self, eng, fn, waits=(), sem=None, dma=False, force_inc=False):
        sem = sem or eng
        self.sem_names.add(sem)
        o = _Op(fn, eng, sem, 16 if dma else 1, waits)
        o.needed = force_inc or dma
        self.ops[eng].append(o)
        return o

    def finalize(self):
        for e in ENGS:
            for o in self.ops[e]:
                if e != "vector":
                    o.waits = [w for w in o.waits
                               if w.eng != e or w.inc_amt == 16]
                for w in o.waits:
                    w.needed = True
        counts = {s: 0 for s in self.sem_names}
        self.dma_sems = {}
        for e in ENGS:
            for o in self.ops[e]:
                if o.needed:
                    counts[o.sem] += o.inc_amt
                    o.count = counts[o.sem]
                if o.inc_amt == 16:
                    self.dma_sems[o.sem] = e
        self.final_counts = counts

    def emit(self, eng_name, eng, sems):
        satisfied = {}
        for o in self.ops[eng_name]:
            best = {}
            for w in o.waits:
                assert w.count is not None, "wait on op without inc"
                if w.count > best.get(w.sem, -1):
                    best[w.sem] = w.count
            for s in sorted(best):
                if best[s] > satisfied.get(s, -1):
                    eng.wait_ge(sems[s], best[s])
                    satisfied[s] = best[s]
            ins = o.fn(eng)
            if o.needed:
                ins.then_inc(sems[o.sem], o.inc_amt)
        for s, e in self.dma_sems.items():
            if e == eng_name and self.final_counts[s] > 0:
                eng.wait_ge(sems[s], self.final_counts[s])


def build_program(taps=None, stage=6, nphase=2):
    taps = taps or {}
    nc = bass.Bass("TRN2", target_bir_lowering=False, debug=False,
                   detect_race_conditions=False)

    dx = nc.dram_tensor("x", [NTOK, H], F32, kind="ExternalInput").ap()
    dcos = nc.dram_tensor("cos", [128, NTOK], F32, kind="ExternalInput").ap()
    dsin = nc.dram_tensor("sin", [128, NTOK], F32, kind="ExternalInput").ap()
    dwq = nc.dram_tensor("wq_t", [8, 128, 32, 512], F32, kind="ExternalInput").ap()
    dwo = nc.dram_tensor("wo_t", [8, 128, 32, 512], F32, kind="ExternalInput").ap()
    dgate = nc.dram_tensor("gate_t", [NIT, 128, 32, 128], I8, kind="ExternalInput").ap()
    dup = nc.dram_tensor("up_t", [NIT, 128, 32, 128], I8, kind="ExternalInput").ap()
    ddown = nc.dram_tensor("down_t", [32, 128, NIT, 128], I8, kind="ExternalInput").ap()
    dgs = nc.dram_tensor("gs", [128, NIT], F32, kind="ExternalInput").ap()
    dus = nc.dram_tensor("us", [128, NIT], F32, kind="ExternalInput").ap()
    dds = nc.dram_tensor("ds", [128, 32], F32, kind="ExternalInput").ap()
    dkv = nc.dram_tensor("kv", [ROWS_KV, 128], F32, kind="ExternalInput").ap()
    dbt = nc.dram_tensor("bt_rep", [128, 4], I32, kind="ExternalInput").ap()
    djv = nc.dram_tensor("jvec", [128, 1], F32, kind="ExternalInput").ap()
    dident = nc.dram_tensor("ident", [128, 128], BF16, kind="ExternalInput").ap()
    dmt = nc.dram_tensor("mt_rope", [128, 128], BF16, kind="ExternalInput").ap()
    dones = nc.dram_tensor("ones", [128, 128], BF16, kind="ExternalInput").ap()
    dout = nc.dram_tensor("out", [NTOK, H], F32, kind="ExternalOutput").ap()
    dtaps = {k: nc.dram_tensor("tap_" + k, shp, dt, kind="ExternalOutput").ap()
             for k, (shp, dt) in taps.items()}

    def sbt(name, offset, shape, dtype):
        return nc.alloc_sbuf_tensor_at(name, shape, dtype, offset=offset + 128)

    cos_ph = sbt("cos_ph", 16384, [128, 512], F32)
    sin_ph = sbt("sin_ph", 18432, [128, 512], F32)
    ident = sbt("ident", 20480, [128, 128], BF16)
    mtr = sbt("mtr", 20736, [128, 128], BF16)
    ones = sbt("ones", 20992, [128, 128], BF16)
    cstf = sbt("cstf", 21504, [128, 16], F32)
    gs = sbt("gs", 21632, [128, NIT], F32)
    us = sbt("us", 21984, [128, NIT], F32)
    gu = sbt("gu", 22336, [128, NIT], F32)
    ds = sbt("ds", 22688, [128, 32], F32)
    bt_rep = sbt("bt_rep", 22816, [128, 4], I32)
    jvec = sbt("jvec", 23072, [128, 1], F32)
    idxk2 = sbt("idxk2", 23360, [128, 8], I32)
    idxv2 = sbt("idxv2", 23872, [128, 8], I32)
    ltb = sbt("ltb", 24384, [128, 24], F32)
    dacc = sbt("dacc", 24480, [128, 2 * 512], F32)
    qrope = sbt("qrope", 28576, [128, 4 * 512], BF16)
    xnbufA = sbt("xnbufA", 24480, [128, 4096], BF16)
    qb = sbt("qb", 32672, [128, 512], BF16)
    rt1 = sbt("rt1", 33696, [128, 512], F32)
    rt2 = sbt("rt2", 35744, [128, 512], F32)
    recip = sbt("recip", 37792, [128, 512], F32)
    utmp = sbt("utmp", 24480, [128, 2 * 512], F32)
    stmp = sbt("stmp", 28576, [128, 2 * 512], F32)
    pbuf = sbt("pbuf", 32672, [128, 2 * 512], BF16)

    xnT = sbt("xnT", 39840, [128, 32 * 512], BF16)
    dslabA = sbt("dslabA", 39840, [128, NIT * 128], BF16)
    ctx = sbt("ctx", 72608, [128, 32 * 512], BF16)
    sqscr = sbt("sqscr", 72608, [128, 4096], F32)
    gslab = sbt("gslab", 72608, [128, 2 * 32 * 128], BF16)
    uslab = sbt("uslab", 88992, [128, 2 * 32 * 128], BF16)
    dslabB = sbt("dslabB", 72608, [128, NIT * 128], BF16)
    xmt = sbt("xmt", 94624, [128, 2 * 512], F32)
    ostg = sbt("ostg", 98720, [128, 2 * 512], F32)
    wqslab = sbt("wqslab", 105376, [128, 32 * 512], BF16)
    s_h = sbt("s_h", 105376, [128, 32 * 512], BF16)
    KT = [sbt("KT0", 138144, [128, TKV], BF16), sbt("KT1", 170912, [128, TKV], BF16)]
    VH = [sbt("V0", 154528, [128, TKV], BF16), sbt("V1", 187296, [128, TKV], BF16)]
    kst = sbt("kst", 203680, [128, 32 * 128], BF16)
    expr = sbt("expr", 211872, [128, 8 * 512], BF16)
    x_tA = sbt("x_tA", 170912, [128, 4096], F32)
    x_tB = sbt("x_tB", 187296, [128, 4096], F32)
    xnbufB = sbt("xnbufB", 211872, [128, 4096], BF16)
    wo_sl = [sbt("woA", 138144, [128, 32 * 512], BF16),
             sbt("woB", 170912, [128, 32 * 512], BF16)]
    h_half = sbt("h_half", 138144, [128, NIT * 512], BF16)
    idxf = sbt("idxf", 226208, [128, 4], F32)
    idxf2 = sbt("idxf2", 226464, [128, 4], F32)
    idxf3 = sbt("idxf3", 226720, [128, 4], F32)
    datmp = sbt("datmp", 220064, [128, 3 * 512], BF16)
    dacb = sbt("dacb", 223136, [128, 512], BF16)
    lnscr = sbt("lnscr", 224160, [128, 512], F32)

    pb = [nc.alloc_psum_tensor(f"pb{i}", [128, 512], F32) for i in range(2)]
    SS = [nc.alloc_psum_tensor(f"ps{i}", [128, 1024], F32) for i in range(2)]
    pq6 = nc.alloc_psum_tensor("pq6", [128, 512], F32)
    pr7 = nc.alloc_psum_tensor("pr7", [128, 512], F32)
    nc.sbuf_base = 227104

    P = Sched()
    bank_guard = {}

    def bw(k):
        return tuple(bank_guard.get(k, ()))

    def bs(k, *toks):
        bank_guard[k] = [t for t in toks if t is not None]

    def OP(eng, fn, waits=(), sem=None, dma=False, force_inc=False):
        return P.op(eng, fn, waits, sem=sem, dma=dma, force_inc=force_inc)

    def vop(fn, waits=()):
        return OP("vector", fn, waits)

    ld = []
    for dst, src in ((ident, dident), (mtr, dmt), (ones, dones), (gs, dgs),
                     (us, dus), (ds, dds), (bt_rep, dbt), (jvec, djv)):
        ld.append(OP("sync", lambda e, d=dst, s=src: e.dma_start(d[:, :], s),
                     sem="dS", dma=True))
    cl = ld[-1]
    CF = {"inv4096": 0, "eps": 1, "expbias": 2, "rsqrtd": 3, "neg1": 4, "negh": 15}
    m0 = None
    for k, v in (("inv4096", 1.0 / 4096), ("eps", 1e-5), ("expbias", -8.0),
                 ("rsqrtd", float(1.0 / np.sqrt(128.0))), ("neg1", -1.0),
                 ("negh", -0.5)):
        m0 = vop(lambda e, c=CF[k], v=v: e.memset(cstf[:, c:c + 1], v))
    for g in range(8):
        m0 = vop(lambda e, g=g: e.memset(cstf[:, 7 + g:8 + g], float(g * 16)))
    cvt = vop(lambda e: e.tensor_copy(idxf[:, :], bt_rep[:, :]), waits=(cl, m0))
    c256 = vop(lambda e: e.memset(cstf[:, 5:6], 256.0))
    c128 = vop(lambda e: e.memset(cstf[:, 6:7], 128.0))
    idxb = vop(lambda e: e.tensor_single_scalar(idxf2[:, :], idxf[:, :],
                                                cstf[:, 5:6], ALU.mult),
               waits=(cvt, c256, c128))
    gu_t = vop(lambda e: e.tensor_mul(gu[:, :], gs[:, :], us[:, :]), waits=(cl,))
    consts_ready = vop(lambda e: e.tensor_copy(sqscr[:, 0:1], cstf[:, 0:1]),
                       waits=(idxb, gu_t))

    idx_guard = {}

    def make_idx(g, waits):
        b = (g % 2) * 4
        iw = idx_guard.get(g % 2)
        t0 = vop(lambda e, g=g: e.tensor_single_scalar(
            idxf3[:, :], idxf2[:, :], cstf[:, 7 + g:8 + g], ALU.add),
            waits=tuple(waits) + (consts_ready, iw))
        t1 = vop(lambda e, b=b: e.tensor_copy(idxk2[:, b:b + 4], idxf3[:, :]),
                 waits=(t0,))
        tv = vop(lambda e: e.tensor_single_scalar(
            idxf[:, :], idxf3[:, :], cstf[:, 6:7], ALU.add), waits=(t0,))
        t2 = vop(lambda e, b=b: e.tensor_copy(idxv2[:, b:b + 4], idxf[:, :]),
                 waits=(tv, t1))
        return t2

    kst_guard = {}
    gather_done = {}

    def plan_gathers(g, waits):
        b2 = g % 2
        ib = b2 * 4
        last_tr = None
        last_v = None
        for q in range(4):
            slot = q % 2
            war = kst_guard.get(slot)
            gk = OP("gpsimd", lambda e, q=q, ib=ib, slot=slot: e.indirect_dma_start(
                kst[:, slot * 2048:(slot + 1) * 2048], None, dkv,
                IndirectOffsetOnAxis(idxk2[:, ib + q:ib + q + 1], 0)),
                waits=tuple(waits) + ((war,) if war else ()), sem="dG", dma=True)
            for j in range(16):
                c = q * 16 + j
                last_tr = OP("sync", lambda e, c=c, j=j, b2=b2, slot=slot:
                             e.dma_start_transpose(
                    KT[b2][:, c * 128:(c + 1) * 128],
                    kst[:, slot * 2048 + j * 128: slot * 2048 + (j + 1) * 128]),
                    waits=(gk,), sem="dT", dma=True)
            kst_guard[slot] = last_tr
        for q in range(4):
            last_v = OP("gpsimd", lambda e, q=q, b2=b2, ib=ib: e.indirect_dma_start(
                VH[b2][:, q * 2048:(q + 1) * 2048], None, dkv,
                IndirectOffsetOnAxis(idxv2[:, ib + q:ib + q + 1], 0)),
                waits=tuple(waits), sem="dG", dma=True)
        idx_guard[b2] = last_v
        gather_done[g] = (last_tr, last_v)

    prev_phase_done = (consts_ready,)
    pe_last = {}
    out_tokens = []

    for ph in range(nphase):
        pw = tuple(prev_phase_done)
        lw = () if ph == 0 else pw

        xts_ln = {}
        xtile = [x_tA, x_tB]

        def ln_load(rt, q0, s0, q1, s1_, waits, ph=ph):
            buf = xtile[rt % 2]
            a = OP(q0, lambda e, rt=rt, buf=buf, ph=ph: e.dma_start(
                buf[:, 0:2048],
                dx[ph * 512 + rt * 128: ph * 512 + (rt + 1) * 128, 0:2048]),
                waits=waits, sem=s0, dma=True)
            b = OP(q1, lambda e, rt=rt, buf=buf, ph=ph: e.dma_start(
                buf[:, 2048:4096],
                dx[ph * 512 + rt * 128: ph * 512 + (rt + 1) * 128, 2048:4096]),
                waits=waits, sem=s1_, dma=True)
            xts_ln[rt] = (a, b)

        ln_load(0, "sync", "dS", "scalar", "dC", lw)
        ln_load(1, "gpsimd", "dG", "scalar", "dC", lw)
        ck = OP("sync", lambda e, ph=ph: e.dma_start(
            cos_ph[:, :], dcos[:, ph * 512:(ph + 1) * 512]), waits=pw,
            sem="dS", dma=True)
        sk = OP("sync", lambda e, ph=ph: e.dma_start(
            sin_ph[:, :], dsin[:, ph * 512:(ph + 1) * 512]), waits=pw,
            sem="dS", dma=True)

        it0 = make_idx(0, pw)
        wq_tok = OP("gpsimd", lambda e: e.dma_start(wqslab[:, :], dwq[0]),
                    waits=pw, sem="dG", dma=True)
        plan_gathers(0, pw + (it0,))

        xn_evac = {}
        ln_xn = {}
        ln_rs = {}
        last_tt = {}
        for rt in range(4):
            buf = xtile[rt % 2]
            xnb = xnbufA if rt % 2 == 0 else xnbufB
            la, lb2 = xts_ln[rt]
            c = rt * 6
            rs = vop(lambda e, buf=buf, c=c: e.reduce_sum(
                ltb[:, c:c + 1], buf[:, :], AX), waits=(la, lb2))
            ln_rs[rt] = rs
            sq = OP("scalar", lambda e, buf=buf, c=c: e.activation(
                sqscr[:, :], buf[:, :], AF.Square, accum_out=ltb[:, c + 1:c + 2]),
                waits=(la, lb2, consts_ready))
            s1 = vop(lambda e, c=c: e.tensor_single_scalar(
                ltb[:, c + 2:c + 3], ltb[:, c:c + 1], cstf[:, 0:1], ALU.mult),
                waits=(rs, sq))
            s2 = vop(lambda e, c=c: e.tensor_single_scalar(
                ltb[:, c + 3:c + 4], ltb[:, c + 1:c + 2], cstf[:, 0:1], ALU.mult),
                waits=(s1,))
            s3 = vop(lambda e, c=c: e.tensor_mul(
                ltb[:, c:c + 1], ltb[:, c + 2:c + 3], ltb[:, c + 2:c + 3]),
                waits=(s1,))
            s4 = vop(lambda e, c=c: e.tensor_sub(
                ltb[:, c + 1:c + 2], ltb[:, c + 3:c + 4], ltb[:, c:c + 1]),
                waits=(s2, s3))
            s5 = vop(lambda e, c=c: e.tensor_single_scalar(
                ltb[:, c + 3:c + 4], ltb[:, c + 1:c + 2], cstf[:, 1:2], ALU.add),
                waits=(s4,))
            s6 = vop(lambda e, c=c: e.tensor_copy(
                ltb[:, c + 1:c + 2], ltb[:, c + 3:c + 4]), waits=(s5,))
            s7a = OP("scalar", lambda e, c=c: e.activation(
                ltb[:, c:c + 1], ltb[:, c + 1:c + 2], AF.Ln), waits=(s6,))
            s7 = OP("scalar", lambda e, c=c: e.activation(
                ltb[:, c + 4:c + 5], ltb[:, c:c + 1], AF.Exp,
                scale=cstf[:, 15:16]), waits=(s7a,))
            s8 = vop(lambda e, c=c: e.tensor_mul(
                ltb[:, c + 1:c + 2], ltb[:, c + 2:c + 3], ltb[:, c + 4:c + 5]),
                waits=(s7, s1))
            s9 = vop(lambda e, c=c: e.tensor_single_scalar(
                ltb[:, c + 5:c + 6], ltb[:, c + 1:c + 2], cstf[:, 4:5], ALU.mult),
                waits=(s8,))
            ttw = (last_tt[rt - 2],) if rt >= 2 else ()
            xn = OP("scalar", lambda e, buf=buf, xnb=xnb, c=c: e.activation(
                xnb[:, :], buf[:, :], AF.Identity,
                bias=ltb[:, c + 5:c + 6], scale=ltb[:, c + 4:c + 5]),
                waits=(s9,) + ttw)
            ln_xn[rt] = xn
            tt = None
            for fb in range(4):
                sj = fb % 2
                for j in range(8):
                    ft = fb * 8 + j
                    tt = OP("tensor", lambda e, ft=ft, j=j, sj=sj, xnb=xnb: e.transpose(
                        SS[sj].ap().bitcast(BF16)[:, j * 128:(j + 1) * 128],
                        xnb[:, ft * 128:(ft + 1) * 128], ident[:, :]),
                        waits=(xn, cl) + bw("S%d" % sj))
                ev = vop(lambda e, fb=fb, rt=rt, sj=sj: e.tensor_copy(
                    AP(xnT.ap().tensor, fb * 8 * 512 + rt * 128,
                       [[32 * 512, 128], [512, 8], [1, 128]]),
                    AP(SS[sj].ap().bitcast(BF16).tensor, 0,
                       [[2048, 128], [128, 8], [1, 128]])),
                    waits=(tt,))
                bs("S%d" % sj, ev)
                xn_evac[(rt, fb)] = ev
            last_tt[rt] = tt
            if rt < 2:
                ln_load(rt + 2, "scalar", "dC", "scalar", "dC",
                        (ln_xn[rt], ln_rs[rt]))
        xnT_ready = tuple(xn_evac.values())
        ln_xt_done = (ln_xn[2], ln_rs[2], ln_xn[3], ln_rs[3])
        if "xnT" in taps and ph == 0:
            OP("sync", lambda e: e.dma_start(dtaps["xnT"], xnT[:, :]),
               waits=xnT_ready, sem="dS", dma=True)
        if "xnT1" in taps and ph == 1:
            OP("sync", lambda e: e.dma_start(dtaps["xnT1"], xnT[:, :]),
               waits=xnT_ready, sem="dS", dma=True)
        if "ltb1" in taps and ph == 1:
            OP("sync", lambda e: e.dma_start(dtaps["ltb1"], ltb[:, :]),
               waits=(ln_xn[0], ln_xn[1], ln_xn[2], ln_xn[3]), sem="dS", dma=True)

        if stage < 2:
            prev_phase_done = (last_xn,)
            continue
        ctx_ev = {}
        sweep_end = {}
        rope_tok = {}
        qp_tok = {}
        sm_last = {}
        dacc_guard = {}
        dacb_guard = [None]
        ring_guard = {}
        wq_cur = {0: wq_tok}

        def qproj_rope(gg, qh):
            mm = None
            for k in range(32):
                mm = OP("tensor", lambda e, k=k, qh=qh: e.matmul(
                    pq6[:, :],
                    wqslab[:, k * 512 + qh * 128: k * 512 + (qh + 1) * 128],
                    xnT[:, k * 512:(k + 1) * 512],
                    start=(k == 0), stop=(k == 31)),
                    waits=(wq_cur[gg],) + xnT_ready + bw("b6"))
            finish_qproj(gg, qh, mm)()

        def finish_qproj(gg, qh, mm):
            qp_tok[(gg, qh)] = mm
            qbc = OP("scalar", lambda e: e.copy(qb[:, :], pq6[:, :]), waits=(mm,))

            def rot_rest():
                rot = OP("tensor", lambda e: e.matmul(
                    pr7[:, :], mtr[:, :], qb[:, :], start=True, stop=True),
                    waits=(qbc, cl) + bw("b7"))
                v1 = vop(lambda e: e.tensor_mul(rt1[:, :], pq6[:, :], cos_ph[:, :]),
                         waits=(mm, sk, ck, qbc))
                bs("b6", qbc, v1)
                v2 = vop(lambda e: e.tensor_mul(rt2[:, :], pr7[:, :], sin_ph[:, :]),
                         waits=(rot,))
                bs("b7", v2)
                slotw = sm_last.get((gg - 1, qh))
                v3 = vop(lambda e, qh=qh: e.tensor_add(
                    qrope[:, qh * 512:(qh + 1) * 512], rt1[:, :], rt2[:, :]),
                    waits=(v1, v2) + ((slotw,) if slotw else ()))
                rope_tok[(gg, qh)] = v3
            return rot_rest

        filler = []
        tail_q = []
        tail_dq = []
        rotq = []

        def queue_qproj(gg, qh):
            st = {"mm": None}
            for k in range(32):
                def thunk(k=k, gg=gg, qh=qh, st=st):
                    st["mm"] = OP("tensor", lambda e, k=k, qh=qh: e.matmul(
                        pq6[:, :],
                        wqslab[:, k * 512 + qh * 128: k * 512 + (qh + 1) * 128],
                        xnT[:, k * 512:(k + 1) * 512],
                        start=(k == 0), stop=(k == 31)),
                        waits=(wq_cur[gg],) + xnT_ready + bw("b6"))
                    if k == 31:
                        rotq.append([1, finish_qproj(gg, qh, st["mm"])])
                filler.append(thunk)

        for qh in range(3):
            qproj_rope(0, qh)
        if ph == 0 and "qrope" in taps:
            OP("sync", lambda e: e.dma_start(dtaps["qrope"], qrope[:, 0:3 * 512]),
               waits=(rope_tok[(0, 2)],), sem="dS", dma=True)

        ngroups = 8 if stage >= 3 else 1
        for g in range(ngroups):
            b2 = g % 2
            ktr_last, vg_last = gather_done[g]
            if g == 0 and ph == 0:
                if "kt0" in taps:
                    OP("sync", lambda e: e.dma_start(dtaps["kt0"], KT[0][:, :]),
                       waits=(ktr_last,), sem="dS", dma=True)
                if "v0" in taps:
                    OP("sync", lambda e: e.dma_start(dtaps["v0"], VH[0][:, :]),
                       waits=(vg_last,), sem="dS", dma=True)

            for qh in range(4):
                if qh == 1 and g < ngroups - 1:
                    bfr = sweep_end.get(g - 1)
                    wq_cur[g + 1] = OP("gpsimd", lambda e, g=g: e.dma_start(
                        wqslab[:, :], dwq[g + 1]), waits=(qp_tok[(g, 3)],),
                        sem="dG", dma=True)
                    itk = make_idx(g + 1, ())
                    lnw = ln_xt_done if g == 0 else ()
                    plan_gathers(g + 1, (itk,) + ((bfr,) if bfr else ()) + lnw)
                tgt = g * 4 + qh + 3
                if tgt < ngroups * 4:
                    queue_qproj(tgt // 4, tgt % 4)
                ctxk = "b%d" % (qh % 2)
                ctxp = pb[qh % 2]
                dac = dacc[:, (qh % 2) * 512:(qh % 2 + 1) * 512]
                da = None
                ex_tok = {}
                cx_tok = {}
                defer_cx = []
                cx_l = None
                sm_l = None
                for c2 in range(34):
                  if c2 < 32:
                    j = c2 % 2
                    sk_ = "S%d" % j
                    sms = []
                    for hh in range(2):
                        c = 2 * c2 + hh
                        sm = OP("tensor", lambda e, c=c, qh=qh, j=j, hh=hh, b2=b2: e.matmul(
                            SS[j][:, hh * 512:(hh + 1) * 512],
                            KT[b2][:, c * 128:(c + 1) * 128],
                            qrope[:, qh * 512:(qh + 1) * 512],
                            start=True, stop=True),
                            waits=(rope_tok[(g, qh)], ktr_last) + bw(sk_))
                        sms.append(sm)
                    sm_l = sms[-1]
                    slot = c2 % 4
                    rw = ring_guard.get(slot, ())
                    ex = OP("scalar", lambda e, c2=c2, j=j, slot=slot: e.activation(
                        expr[:, slot * 1024:(slot + 1) * 1024], SS[j][:, :],
                        AF.Exp, bias=cstf[:, 2:3], scale=cstf[:, 3:4]),
                        waits=(sms[1], consts_ready) + tuple(rw))
                    ex_tok[c2] = ex
                    bs(sk_, ex)
                  if c2 >= 2:
                    p2 = c2 - 2
                    slotp = p2 % 4
                    exp_ = ex_tok[p2]
                    cxs = []
                    for hh in range(2):
                        c = 2 * p2 + hh
                        cxw = bw(ctxk) if c == 0 else ()
                        cx = OP("tensor", lambda e, c=c, hh=hh, slot=slotp, b2=b2,
                                ctxp=ctxp: e.matmul(
                            ctxp[:, :], VH[b2][:, c * 128:(c + 1) * 128],
                            expr[:, slot * 1024 + hh * 512: slot * 1024 + (hh + 1) * 512],
                            start=(c == 0), stop=(c == 63)),
                            waits=(exp_, vg_last) + cxw)
                        cxs.append(cx)
                    cx_l = cxs[-1]
                    cx_tok[p2] = cx_l
                    for (sslot, cc2) in defer_cx:
                        if cc2 == p2:
                            ring_guard[sslot].append(cx_l)
                    defer_cx = [d for d in defer_cx if d[1] != p2]
                  if c2 < 32 and c2 % 2 == 1:
                        sa = (c2 - 1) % 4
                        sb = c2 % 4
                        q1 = vop(lambda e, sa=sa: e.tensor_add(
                            datmp[:, 0:512], expr[:, sa * 1024:sa * 1024 + 512],
                            expr[:, sa * 1024 + 512:(sa + 1) * 1024]),
                            waits=(ex_tok[c2 - 1],))
                        q2 = vop(lambda e, sb=sb: e.tensor_add(
                            datmp[:, 512:1024], expr[:, sb * 1024:sb * 1024 + 512],
                            expr[:, sb * 1024 + 512:(sb + 1) * 1024]),
                            waits=(ex,))
                        q3 = vop(lambda e: e.tensor_add(
                            datmp[:, 1024:1536], datmp[:, 0:512],
                            datmp[:, 512:1024]), waits=(q1, q2))
                        if c2 == 1:
                            dgw = dacc_guard.get(qh % 2)
                            da = vop(lambda e, dac=dac: e.tensor_copy(
                                dac, datmp[:, 1024:1536]),
                                waits=(q3,) + ((dgw,) if dgw else ()))
                        else:
                            da = vop(lambda e, dac=dac: e.tensor_add(
                                dac, dac, datmp[:, 1024:1536]), waits=(q3, da))
                        ring_guard[sa] = [q1]
                        defer_cx.append((sa, c2 - 1))
                        ring_guard[sb] = [q2]
                        defer_cx.append((sb, c2))
                  if tail_dq:
                    tail_dq.pop(0)()
                  if rotq:
                    if rotq[0][0] <= 0:
                        rotq.pop(0)[1]()
                    else:
                        rotq[0][0] -= 1
                  if tail_q and c2 >= 8:
                    tail_q.pop(0)()
                  if filler and c2 >= 16:
                    filler.pop(0)()
                    if filler:
                        filler.pop(0)()
                sm_last[(g, qh)] = sm_l

                box = {}

                def denom_dct(box=box, qh=qh, dac=dac, da=da):
                    dbw = dacb_guard[0]
                    box["dct"] = vop(lambda e, dac=dac: e.tensor_copy(
                        dacb[:, :], dac), waits=(da,) + ((dbw,) if dbw else ()))
                    dacc_guard[qh % 2] = box["dct"]

                def denom_tail(box=box, g=g, qh=qh, ctxp=ctxp, cx_l=cx_l):
                    rd = OP("tensor", lambda e: e.matmul(
                        pr7[:, :], ones[:, :], dacb[:, :], start=True, stop=True),
                        waits=(box["dct"], cl) + bw("b7"))
                    dacb_guard[0] = rd
                    rca = OP("scalar", lambda e: e.activation(
                        lnscr[:, :], pr7[:, :], AF.Ln), waits=(rd,))
                    rc = OP("scalar", lambda e: e.activation(
                        recip[:, :], lnscr[:, :], AF.Exp, scale=cstf[:, 4:5]),
                        waits=(rca,))
                    bs("b7", rca)
                    cev = vop(lambda e, g=g, qh=qh, ctxp=ctxp: e.tensor_mul(
                        ctx[:, (g * 4 + qh) * 512:(g * 4 + qh + 1) * 512],
                        ctxp[:, :], recip[:, :]), waits=(rc, cx_l))
                    ctx_ev[(g, qh)] = cev
                    bs("b%d" % (qh % 2), cev)
                tail_dq.append(denom_dct)
                tail_q.append(denom_tail)
                sweep_end[g] = cx_l
                while rotq:
                    rotq.pop(0)[1]()
                if qh == 3 and g == ngroups - 1:
                    while tail_dq:
                        tail_dq.pop(0)()
                    while tail_q:
                        tail_q.pop(0)()
                    while filler:
                        filler.pop(0)()

        ctx_all = tuple(ctx_ev[(g, qh)] for g in range(ngroups) for qh in range(4))
        if stage < 3:
            prev_phase_done = ctx_all
            continue
        if "ctx" in taps and ph == 0:
            OP("sync", lambda e: e.dma_start(dtaps["ctx"], ctx[:, :]),
               waits=ctx_all, sem="dS", dma=True)

        wo_tok = [None, None]
        wo_tok[0] = OP("gpsimd", lambda e: e.dma_start(wo_sl[0][:, :], dwo[0]),
                       waits=(sweep_end[6],), sem="dG", dma=True)
        wo_tok[1] = OP("gpsimd", lambda e: e.dma_start(wo_sl[1][:, :], dwo[1]),
                       waits=(sweep_end[7],), sem="dG", dma=True)
        s_ev = {}
        womm = {}
        for og in range(8):
            if og >= 2:
                wo_tok[og % 2] = OP("gpsimd", lambda e, og=og: e.dma_start(
                    wo_sl[og % 2][:, :], dwo[og]),
                    waits=(womm[(og - 2, 3)],), sem="dG", dma=True)
            for m in range(4):
                sj = m % 2
                bk = "S%d" % sj
                mm = None
                for k in range(32):
                    mm = OP("tensor", lambda e, k=k, m=m, og=og, sj=sj: e.matmul(
                        SS[sj][:, 0:512],
                        wo_sl[og % 2][:, k * 512 + m * 128: k * 512 + (m + 1) * 128],
                        ctx[:, k * 512:(k + 1) * 512],
                        start=(k == 0), stop=(k == 31)),
                        waits=(wo_tok[og % 2],) + ctx_all + bw(bk))
                womm[(og, m)] = mm
                ev = OP("scalar", lambda e, og=og, m=m, sj=sj: e.copy(
                    s_h[:, (og * 4 + m) * 512:(og * 4 + m + 1) * 512],
                    SS[sj][:, 0:512]), waits=(mm,))
                s_ev[(og, m)] = ev
                bs(bk, ev)
        if "s" in taps and ph == 0:
            OP("sync", lambda e: e.dma_start(dtaps["s"], s_h[:, :]),
               waits=tuple(s_ev.values()), sem="dS", dma=True)
        s_last = s_ev[(7, 3)]

        gsl_tok = {}
        hev = {}
        gmm = {}
        for it in range(86):
            sl = it % 2
            gss = gslab[:, sl * 4096:(sl + 1) * 4096]
            uss = uslab[:, sl * 4096:(sl + 1) * 4096]
            if it < 2:
                ldw = (s_last, womm[(7, 3)])
            else:
                ldw = (gmm[it - 2],)
            gt = OP("gpsimd", lambda e, it=it, gss=gss: e.dma_start(gss, dgate[it]),
                    waits=ldw, sem="dG", dma=True)
            ut = OP("gpsimd", lambda e, it=it, uss=uss: e.dma_start(uss, dup[it]),
                    waits=ldw, sem="dG", dma=True)
            gsl_tok[it] = (gt, ut)
            gk_ = "b%d" % sl
            up = pq6 if sl == 0 else pr7
            uk = "b6" if sl == 0 else "b7"
            mmg = mmu = None
            for k in range(32):
                mmg = OP("tensor", lambda e, k=k, sl=sl, gss=gss: e.matmul(
                    pb[sl][:, :], gss[:, k * 128:(k + 1) * 128],
                    xnT[:, k * 512:(k + 1) * 512],
                    start=(k == 0), stop=(k == 31)),
                    waits=(gt,) + xnT_ready + bw(gk_))
            for k in range(32):
                mmu = OP("tensor", lambda e, k=k, up=up, uss=uss: e.matmul(
                    up[:, :], uss[:, k * 128:(k + 1) * 128],
                    xnT[:, k * 512:(k + 1) * 512],
                    start=(k == 0), stop=(k == 31)),
                    waits=(ut,) + bw(uk))
            gmm[it] = mmu
            uev = OP("scalar", lambda e, sl=sl, up=up: e.copy(
                utmp[:, sl * 512:(sl + 1) * 512], up[:, :]), waits=(mmu,))
            he = vop(lambda e, it=it, sl=sl: e.scalar_tensor_tensor(
                h_half[:, it * 512:(it + 1) * 512], pb[sl][:, :], gu[:, it:it + 1],
                utmp[:, sl * 512:(sl + 1) * 512], ALU.mult, ALU.mult),
                waits=(uev, mmg, consts_ready))
            hev[it] = he
            bs(gk_, he)
            bs(uk, uev, he)
        h_all = tuple(hev.values())
        if "h" in taps and ph == 0:
            OP("sync", lambda e: e.dma_start(
                dtaps["h"], h_half[:, 0:taps["h"][0][1]]),
               waits=h_all, sem="dS", dma=True)

        dsl = [dslabA, dslabB]
        dtok = [None, None]
        comb = {}
        dmm = {}
        fin_q = []
        out_box = [None]
        for mt in range(32):
            sl = mt % 2
            if mt < 2:
                ldw = (gmm[85], hev[85], gsl_tok[85][sl])
            else:
                ldw = (dmm[mt - 2],)
            dtok[sl] = OP("gpsimd", lambda e, mt=mt, sl=sl: e.dma_start(
                dsl[sl][:, :], ddown[mt]), waits=ldw, sem="dG", dma=True)
            xw = (comb[(mt - 2, 3)],) if mt >= 2 else \
                (womm[(7, 3)], gmm[85], gsl_tok[85][0], gsl_tok[85][1])
            xts = []
            for tt in range(4):
                xts.append(OP("sync", lambda e, mt=mt, tt=tt, sl=sl, ph=ph: e.dma_start(
                    xmt[:, sl * 512 + tt * 128: sl * 512 + (tt + 1) * 128],
                    dx[ph * 512 + tt * 128: ph * 512 + (tt + 1) * 128,
                       mt * 128:(mt + 1) * 128]),
                    waits=xw, sem="dS", dma=True))
            sj = sl
            bk = "S%d" % sj
            mm = None
            for ki in range(86):
                mm = OP("tensor", lambda e, ki=ki, sj=sj, sl=sl: e.matmul(
                    SS[sj][:, 0:512], dsl[sl][:, ki * 128:(ki + 1) * 128],
                    h_half[:, ki * 512:(ki + 1) * 512],
                    start=(ki == 0), stop=(ki == 85)),
                    waits=(dtok[sl],) + h_all + bw(bk))
                if ki == 12 and fin_q:
                    fin_q.pop(0)()
            dmm[mt] = mm
            si = OP("scalar", lambda e, mt=mt, sl=sl, sj=sj: e.activation(
                stmp[:, sl * 512:(sl + 1) * 512], SS[sj][:, 0:512], AF.Silu,
                scale=ds[:, mt:mt + 1]), waits=(mm, consts_ready))
            bs(bk, si)
            pe = vop(lambda e, mt=mt, sl=sl: e.tensor_mul(
                pbuf[:, sl * 512:(sl + 1) * 512],
                stmp[:, sl * 512:(sl + 1) * 512],
                s_h[:, mt * 512:(mt + 1) * 512]),
                waits=(si, s_ev[(mt // 4, mt % 4)]))
            pe_last[mt] = pe

            def fin(mt=mt, sl=sl, pe=pe, xts=xts, ph=ph):
                tbp = pq6 if sl == 0 else pr7
                tbk = "b6" if sl == 0 else "b7"
                trs = []
                for tt in range(4):
                    trs.append(OP("tensor", lambda e, tt=tt, sl=sl, tbp=tbp: e.transpose(
                        tbp.ap().bitcast(BF16)[:, tt * 128:(tt + 1) * 128],
                        pbuf[:, sl * 512 + tt * 128: sl * 512 + (tt + 1) * 128],
                        ident[:, :]),
                        waits=(pe, cl) + bw(tbk)))
                cas = []
                for tt in range(4):
                    cadd = vop(lambda e, tt=tt, sl=sl, tbp=tbp: e.tensor_add(
                        ostg[:, sl * 512 + tt * 128: sl * 512 + (tt + 1) * 128],
                        tbp.ap().bitcast(BF16)[:, tt * 128:(tt + 1) * 128],
                        xmt[:, sl * 512 + tt * 128: sl * 512 + (tt + 1) * 128]),
                        waits=(trs[3], xts[tt]))
                    comb[(mt, tt)] = cadd
                    cas.append(cadd)
                    out_box[0] = OP("sync", lambda e, mt=mt, tt=tt, sl=sl, ph=ph: e.dma_start(
                        dout[ph * 512 + tt * 128: ph * 512 + (tt + 1) * 128,
                             mt * 128:(mt + 1) * 128],
                        ostg[:, sl * 512 + tt * 128: sl * 512 + (tt + 1) * 128]),
                        waits=(cadd,), sem="dO", dma=True, force_inc=True)
                    out_tokens.append(out_box[0])
                bs(tbk, *cas)
            fin_q.append(fin)
        while fin_q:
            fin_q.pop(0)()

        prev_phase_done = (dmm[31], pe_last[30], pe_last[31],
                           comb[(31, 3)], comb[(30, 3)])

    if out_tokens:
        OP("sync", lambda e: e.dma_start(sqscr[:, 0:16], dx[0:128, 0:16]),
           waits=(out_tokens[-1],), sem="dS", dma=True)

    P.finalize()
    stack = contextlib.ExitStack()
    sems = {s: stack.enter_context(nc.semaphore(f"sem_{s}"))
            for s in sorted(P.sem_names)}
    with nc.Block() as blk:
        blk.sync(lambda e: P.emit("sync", e, sems))
        blk.gpsimd(lambda e: P.emit("gpsimd", e, sems))
        blk.tensor(lambda e: P.emit("tensor", e, sems))
        blk.vector(lambda e: P.emit("vector", e, sems))
        blk.scalar(lambda e: P.emit("scalar", e, sems))
    stack.close()
    return nc



def _byteslice_i8(w):
    w = np.ascontiguousarray(np.asarray(w).astype("<i4", copy=False))
    return w.view(np.int8).reshape(w.shape + (4,))[..., 0]


def _prep(inputs):
    hs = np.ascontiguousarray(np.asarray(inputs["L0_attn_hidden_states"], dtype=np.float32))
    Wq = np.ascontiguousarray(np.asarray(inputs["L0_attn_Wq"], dtype=np.float32))
    Wo = np.ascontiguousarray(np.asarray(inputs["L0_attn_Wo"], dtype=np.float32))
    kv = np.ascontiguousarray(np.asarray(inputs["L0_attn_kv_cache"], dtype=np.float32))
    bt = np.asarray(inputs["L0_attn_block_table"], dtype=np.int32)
    g8 = _byteslice_i8(inputs["L0_ffn_gate_wq"])
    u8 = _byteslice_i8(inputs["L0_ffn_up_wq"])
    d8 = _byteslice_i8(inputs["L0_ffn_down_wq"])
    gsc = np.asarray(inputs["L0_ffn_gate_scale"], dtype=np.float32)
    usc = np.asarray(inputs["L0_ffn_up_scale"], dtype=np.float32)
    dsc = np.asarray(inputs["L0_ffn_down_scale"], dtype=np.float32)

    wq_t = np.ascontiguousarray(Wq.reshape(32, 128, 8, 512).transpose(2, 1, 0, 3))
    wo_t = np.ascontiguousarray(Wo.reshape(32, 128, 8, 512).transpose(2, 1, 0, 3))
    gate_t = np.ascontiguousarray(
        np.ascontiguousarray(g8.T).reshape(32, 128, NIT, 128).transpose(2, 1, 0, 3))
    up_t = np.ascontiguousarray(
        np.ascontiguousarray(u8.T).reshape(32, 128, NIT, 128).transpose(2, 1, 0, 3))
    down_t = np.ascontiguousarray(
        np.ascontiguousarray(d8.T).reshape(NIT, 128, 32, 128).transpose(2, 1, 0, 3))
    gs_t = np.ascontiguousarray(gsc.reshape(NIT, 128).T)
    us_t = np.ascontiguousarray(usc.reshape(NIT, 128).T)
    ds_t = np.ascontiguousarray(dsc.reshape(32, 128).T)
    kvf = kv.reshape(ROWS_KV, 128)
    bt_rep = np.ascontiguousarray(bt.reshape(4, 128).T).astype(np.int32)
    jv = (np.arange(128) % 16).astype(np.float32).reshape(128, 1)
    ident = np.eye(128, dtype=BF)
    mtr = np.zeros((128, 128), dtype=BF)
    for a in range(64):
        mtr[64 + a, a] = -1.0
        mtr[a, 64 + a] = 1.0
    ones = np.ones((128, 128), dtype=BF)

    invf = 1.0 / (10000.0 ** (np.arange(0, D, 2, dtype=np.float64) / D))
    in_maps = []
    xs = hs.reshape(NCORE, NTOK, H)
    for c in range(NCORE):
        pos = (c % 2) * NTOK + np.arange(NTOK, dtype=np.float64)
        ang = pos[:, None] * invf[None, :]
        emb = np.concatenate([ang, ang], axis=1)
        cos_t = np.ascontiguousarray(np.cos(emb).T.astype(np.float32))
        sin_t = np.ascontiguousarray(np.sin(emb).T.astype(np.float32))
        in_maps.append(dict(
            x=xs[c], cos=cos_t, sin=sin_t, wq_t=wq_t, wo_t=wo_t,
            gate_t=gate_t, up_t=up_t, down_t=down_t, gs=gs_t, us=us_t, ds=ds_t,
            kv=kvf, bt_rep=bt_rep, jvec=jv, ident=ident, mt_rope=mtr, ones=ones))
    return in_maps


def kernel(**inputs):
    in_maps = _prep(inputs)
    nc = build_program()
    res = run_bass_kernel_spmd(nc, in_maps, core_ids=list(range(NCORE)))
    out = np.concatenate([np.asarray(r["out"]) for r in res.results], axis=0)
    return np.ascontiguousarray(out.reshape(B, S_, H).astype(np.float32))


# version 18
# speedup vs baseline: 1.0431x; 1.0272x over previous
import contextlib
import numpy as np
import ml_dtypes
import concourse.bass as bass
import concourse.mybir as mybir
from concourse.bass_utils import run_bass_kernel_spmd
from concourse.bass_core_rust import IndirectOffsetOnAxis
from concourse.ap import AP

F32 = mybir.dt.float32
BF16 = mybir.dt.bfloat16
I8 = mybir.dt.int8
I32 = mybir.dt.int32
AF = mybir.ActivationFunctionType
ALU = mybir.AluOpType
BF = ml_dtypes.bfloat16
AX = mybir.AxisListType.X

B, S_, H, NH, KVH, D, II = 4, 2048, 4096, 32, 8, 128, 11008
NB, BLK, BT = 1024, 16, 512
TKV = BT * BLK
NCORE = 8
NTOK = 1024
NIT = II // 128
ROWS_KV = NB * 2 * KVH * BLK

ENGS = ("sync", "gpsimd", "tensor", "vector", "scalar")


class _Op:
    __slots__ = ("fn", "eng", "sem", "inc_amt", "waits", "needed", "count")

    def __init__(self, fn, eng, sem, inc_amt, waits):
        self.fn, self.eng, self.sem, self.inc_amt = fn, eng, sem, inc_amt
        self.waits = [w for w in waits if w is not None]
        self.needed = False
        self.count = None


class Sched:

    def __init__(self):
        self.ops = {e: [] for e in ENGS}
        self.sem_names = set()

    def op(self, eng, fn, waits=(), sem=None, dma=False, force_inc=False):
        sem = sem or eng
        self.sem_names.add(sem)
        o = _Op(fn, eng, sem, 16 if dma else 1, waits)
        o.needed = force_inc or dma
        self.ops[eng].append(o)
        return o

    def finalize(self):
        for e in ENGS:
            for o in self.ops[e]:
                if e != "vector":
                    o.waits = [w for w in o.waits
                               if w.eng != e or w.inc_amt == 16]
                for w in o.waits:
                    w.needed = True
        counts = {s: 0 for s in self.sem_names}
        self.dma_sems = {}
        for e in ENGS:
            for o in self.ops[e]:
                if o.needed:
                    counts[o.sem] += o.inc_amt
                    o.count = counts[o.sem]
                if o.inc_amt == 16:
                    self.dma_sems[o.sem] = e
        self.final_counts = counts

    def emit(self, eng_name, eng, sems):
        satisfied = {}
        for o in self.ops[eng_name]:
            best = {}
            for w in o.waits:
                assert w.count is not None, "wait on op without inc"
                if w.count > best.get(w.sem, -1):
                    best[w.sem] = w.count
            for s in sorted(best):
                if best[s] > satisfied.get(s, -1):
                    eng.wait_ge(sems[s], best[s])
                    satisfied[s] = best[s]
            ins = o.fn(eng)
            if o.needed:
                ins.then_inc(sems[o.sem], o.inc_amt)
        for s, e in self.dma_sems.items():
            if e == eng_name and self.final_counts[s] > 0:
                eng.wait_ge(sems[s], self.final_counts[s])


def build_program(taps=None, stage=6, nphase=2):
    taps = taps or {}
    nc = bass.Bass("TRN2", target_bir_lowering=False, debug=False,
                   detect_race_conditions=False)

    dx = nc.dram_tensor("x", [NTOK, H], F32, kind="ExternalInput").ap()
    dcos = nc.dram_tensor("cos", [128, NTOK], F32, kind="ExternalInput").ap()
    dsin = nc.dram_tensor("sin", [128, NTOK], F32, kind="ExternalInput").ap()
    dwq = nc.dram_tensor("wq_t", [8, 128, 32, 512], F32, kind="ExternalInput").ap()
    dwo = nc.dram_tensor("wo_t", [8, 128, 32, 512], F32, kind="ExternalInput").ap()
    dgate = nc.dram_tensor("gate_t", [NIT, 128, 32, 128], I8, kind="ExternalInput").ap()
    dup = nc.dram_tensor("up_t", [NIT, 128, 32, 128], I8, kind="ExternalInput").ap()
    ddown = nc.dram_tensor("down_t", [32, 128, NIT, 128], I8, kind="ExternalInput").ap()
    dgs = nc.dram_tensor("gs", [128, NIT], F32, kind="ExternalInput").ap()
    dus = nc.dram_tensor("us", [128, NIT], F32, kind="ExternalInput").ap()
    dds = nc.dram_tensor("ds", [128, 32], F32, kind="ExternalInput").ap()
    dkv = nc.dram_tensor("kv", [ROWS_KV, 128], F32, kind="ExternalInput").ap()
    dbt = nc.dram_tensor("bt_rep", [128, 4], I32, kind="ExternalInput").ap()
    djv = nc.dram_tensor("jvec", [128, 1], F32, kind="ExternalInput").ap()
    dident = nc.dram_tensor("ident", [128, 128], BF16, kind="ExternalInput").ap()
    dmt = nc.dram_tensor("mt_rope", [128, 128], BF16, kind="ExternalInput").ap()
    dones = nc.dram_tensor("ones", [128, 128], BF16, kind="ExternalInput").ap()
    dout = nc.dram_tensor("out", [NTOK, H], F32, kind="ExternalOutput").ap()
    dtaps = {k: nc.dram_tensor("tap_" + k, shp, dt, kind="ExternalOutput").ap()
             for k, (shp, dt) in taps.items()}

    def sbt(name, offset, shape, dtype):
        return nc.alloc_sbuf_tensor_at(name, shape, dtype, offset=offset + 128)

    cos_ph = sbt("cos_ph", 16384, [128, 512], F32)
    sin_ph = sbt("sin_ph", 18432, [128, 512], F32)
    ident = sbt("ident", 20480, [128, 128], BF16)
    mtr = sbt("mtr", 20736, [128, 128], BF16)
    ones = sbt("ones", 20992, [128, 128], BF16)
    cstf = sbt("cstf", 21504, [128, 16], F32)
    gs = sbt("gs", 21632, [128, NIT], F32)
    us = sbt("us", 21984, [128, NIT], F32)
    gu = sbt("gu", 22336, [128, NIT], F32)
    ds = sbt("ds", 22688, [128, 32], F32)
    bt_rep = sbt("bt_rep", 22816, [128, 4], I32)
    jvec = sbt("jvec", 23072, [128, 1], F32)
    idxk2 = sbt("idxk2", 23360, [128, 8], I32)
    idxv2 = sbt("idxv2", 23872, [128, 8], I32)
    ltb = sbt("ltb", 24384, [128, 24], F32)
    dacc = sbt("dacc", 24480, [128, 2 * 512], F32)
    qrope = sbt("qrope", 28576, [128, 4 * 512], BF16)
    xnbufA = sbt("xnbufA", 24480, [128, 4096], BF16)
    qb = sbt("qb", 32672, [128, 512], BF16)
    rt1 = sbt("rt1", 33696, [128, 512], F32)
    rt2 = sbt("rt2", 35744, [128, 512], F32)
    recip = sbt("recip", 37792, [128, 512], F32)
    utmp = sbt("utmp", 24480, [128, 2 * 512], F32)
    stmp = sbt("stmp", 28576, [128, 2 * 512], F32)
    pbuf = sbt("pbuf", 32672, [128, 2 * 512], BF16)

    xnT = sbt("xnT", 39840, [128, 32 * 512], BF16)
    dslabA = sbt("dslabA", 39840, [128, NIT * 128], BF16)
    ctx = sbt("ctx", 72608, [128, 32 * 512], BF16)
    sqscr = sbt("sqscr", 72608, [128, 4096], F32)
    gslab = sbt("gslab", 72608, [128, 2 * 32 * 128], BF16)
    uslab = sbt("uslab", 88992, [128, 2 * 32 * 128], BF16)
    dslabB = sbt("dslabB", 72608, [128, NIT * 128], BF16)
    xmt = sbt("xmt", 94624, [128, 2 * 512], F32)
    ostg = sbt("ostg", 98720, [128, 2 * 512], F32)
    wqslab = sbt("wqslab", 105376, [128, 32 * 512], BF16)
    s_h = sbt("s_h", 105376, [128, 32 * 512], BF16)
    KT = [sbt("KT0", 138144, [128, TKV], BF16), sbt("KT1", 170912, [128, TKV], BF16)]
    VH = [sbt("V0", 154528, [128, TKV], BF16), sbt("V1", 187296, [128, TKV], BF16)]
    kst = sbt("kst", 203680, [128, 32 * 128], BF16)
    expr = sbt("expr", 211872, [128, 8 * 512], BF16)
    x_tA = sbt("x_tA", 170912, [128, 4096], F32)
    x_tB = sbt("x_tB", 187296, [128, 4096], F32)
    xnbufB = sbt("xnbufB", 211872, [128, 4096], BF16)
    wo_sl = [sbt("woA", 138144, [128, 32 * 512], BF16),
             sbt("woB", 170912, [128, 32 * 512], BF16)]
    h_half = sbt("h_half", 138144, [128, NIT * 512], BF16)
    idxf = sbt("idxf", 226208, [128, 4], F32)
    idxf2 = sbt("idxf2", 226464, [128, 4], F32)
    idxf3 = sbt("idxf3", 226720, [128, 4], F32)
    datmp = sbt("datmp", 220064, [128, 3 * 512], BF16)
    dacb = sbt("dacb", 223136, [128, 512], BF16)
    lnscr = sbt("lnscr", 224160, [128, 512], F32)

    pb = [nc.alloc_psum_tensor(f"pb{i}", [128, 512], F32) for i in range(2)]
    SS = [nc.alloc_psum_tensor(f"ps{i}", [128, 1024], F32) for i in range(2)]
    pq6 = nc.alloc_psum_tensor("pq6", [128, 512], F32)
    pr7 = nc.alloc_psum_tensor("pr7", [128, 512], F32)
    nc.sbuf_base = 227104

    P = Sched()
    bank_guard = {}

    def bw(k):
        return tuple(bank_guard.get(k, ()))

    def bs(k, *toks):
        bank_guard[k] = [t for t in toks if t is not None]

    def OP(eng, fn, waits=(), sem=None, dma=False, force_inc=False):
        return P.op(eng, fn, waits, sem=sem, dma=dma, force_inc=force_inc)

    def vop(fn, waits=()):
        return OP("vector", fn, waits)

    ld = []
    for dst, src in ((ident, dident), (mtr, dmt), (ones, dones), (gs, dgs),
                     (us, dus), (ds, dds), (bt_rep, dbt), (jvec, djv)):
        ld.append(OP("sync", lambda e, d=dst, s=src: e.dma_start(d[:, :], s),
                     sem="dS", dma=True))
    cl = ld[-1]
    CF = {"inv4096": 0, "eps": 1, "expbias": 2, "rsqrtd": 3, "neg1": 4, "negh": 15}
    m0 = None
    for k, v in (("inv4096", 1.0 / 4096), ("eps", 1e-5), ("expbias", -8.0),
                 ("rsqrtd", float(1.0 / np.sqrt(128.0))), ("neg1", -1.0),
                 ("negh", -0.5)):
        m0 = vop(lambda e, c=CF[k], v=v: e.memset(cstf[:, c:c + 1], v))
    for g in range(8):
        m0 = vop(lambda e, g=g: e.memset(cstf[:, 7 + g:8 + g], float(g * 16)))
    cvt = vop(lambda e: e.tensor_copy(idxf[:, :], bt_rep[:, :]), waits=(cl, m0))
    c256 = vop(lambda e: e.memset(cstf[:, 5:6], 256.0))
    c128 = vop(lambda e: e.memset(cstf[:, 6:7], 128.0))
    idxb = vop(lambda e: e.tensor_single_scalar(idxf2[:, :], idxf[:, :],
                                                cstf[:, 5:6], ALU.mult),
               waits=(cvt, c256, c128))
    gu_t = vop(lambda e: e.tensor_mul(gu[:, :], gs[:, :], us[:, :]), waits=(cl,))
    consts_ready = vop(lambda e: e.tensor_copy(sqscr[:, 0:1], cstf[:, 0:1]),
                       waits=(idxb, gu_t))

    idx_guard = {}

    def make_idx(g, waits):
        b = (g % 2) * 4
        iw = idx_guard.get(g % 2)
        t0 = vop(lambda e, g=g: e.tensor_single_scalar(
            idxf3[:, :], idxf2[:, :], cstf[:, 7 + g:8 + g], ALU.add),
            waits=tuple(waits) + (consts_ready, iw))
        t1 = vop(lambda e, b=b: e.tensor_copy(idxk2[:, b:b + 4], idxf3[:, :]),
                 waits=(t0,))
        tv = vop(lambda e: e.tensor_single_scalar(
            idxf[:, :], idxf3[:, :], cstf[:, 6:7], ALU.add), waits=(t0,))
        t2 = vop(lambda e, b=b: e.tensor_copy(idxv2[:, b:b + 4], idxf[:, :]),
                 waits=(tv, t1))
        return t2

    kst_guard = {}
    gather_done = {}

    def plan_gathers(g, waits):
        b2 = g % 2
        ib = b2 * 4
        last_tr = None
        last_v = None
        for q in range(4):
            slot = q % 2
            war = kst_guard.get(slot)
            gk = OP("gpsimd", lambda e, q=q, ib=ib, slot=slot: e.indirect_dma_start(
                kst[:, slot * 2048:(slot + 1) * 2048], None, dkv,
                IndirectOffsetOnAxis(idxk2[:, ib + q:ib + q + 1], 0)),
                waits=tuple(waits) + ((war,) if war else ()), sem="dG", dma=True)
            for j in range(16):
                c = q * 16 + j
                last_tr = OP("sync", lambda e, c=c, j=j, b2=b2, slot=slot:
                             e.dma_start_transpose(
                    KT[b2][:, c * 128:(c + 1) * 128],
                    kst[:, slot * 2048 + j * 128: slot * 2048 + (j + 1) * 128]),
                    waits=(gk,), sem="dT", dma=True)
            kst_guard[slot] = last_tr
        for q in range(4):
            last_v = OP("gpsimd", lambda e, q=q, b2=b2, ib=ib: e.indirect_dma_start(
                VH[b2][:, q * 2048:(q + 1) * 2048], None, dkv,
                IndirectOffsetOnAxis(idxv2[:, ib + q:ib + q + 1], 0)),
                waits=tuple(waits), sem="dG", dma=True)
        idx_guard[b2] = last_v
        gather_done[g] = (last_tr, last_v)

    prev_phase_done = (consts_ready,)
    pe_last = {}
    out_tokens = []

    for ph in range(nphase):
        pw = tuple(prev_phase_done)
        lw = () if ph == 0 else pw

        xts_ln = {}
        xtile = [x_tA, x_tB]

        def ln_load(rt, q0, s0, q1, s1_, waits, ph=ph):
            buf = xtile[rt % 2]
            a = OP(q0, lambda e, rt=rt, buf=buf, ph=ph: e.dma_start(
                buf[:, 0:2048],
                dx[ph * 512 + rt * 128: ph * 512 + (rt + 1) * 128, 0:2048]),
                waits=waits, sem=s0, dma=True)
            b = OP(q1, lambda e, rt=rt, buf=buf, ph=ph: e.dma_start(
                buf[:, 2048:4096],
                dx[ph * 512 + rt * 128: ph * 512 + (rt + 1) * 128, 2048:4096]),
                waits=waits, sem=s1_, dma=True)
            xts_ln[rt] = (a, b)

        ln_load(0, "gpsimd", "dG", "scalar", "dC", lw)
        ln_load(1, "sync", "dS", "scalar", "dC", lw)
        ck = OP("sync", lambda e, ph=ph: e.dma_start(
            cos_ph[:, :], dcos[:, ph * 512:(ph + 1) * 512]), waits=pw,
            sem="dS", dma=True)
        sk = OP("sync", lambda e, ph=ph: e.dma_start(
            sin_ph[:, :], dsin[:, ph * 512:(ph + 1) * 512]), waits=pw,
            sem="dS", dma=True)

        it0 = make_idx(0, pw)
        wq_tok = OP("gpsimd", lambda e: e.dma_start(wqslab[:, :], dwq[0]),
                    waits=pw, sem="dG", dma=True)
        plan_gathers(0, pw + (it0,))

        xn_evac = {}
        ln_xn = {}
        ln_rs = {}
        last_tt = {}
        for rt in range(4):
            buf = xtile[rt % 2]
            xnb = xnbufA if rt % 2 == 0 else xnbufB
            la, lb2 = xts_ln[rt]
            c = rt * 6
            rs = vop(lambda e, buf=buf, c=c: e.reduce_sum(
                ltb[:, c:c + 1], buf[:, :], AX), waits=(la, lb2))
            ln_rs[rt] = rs
            sq = OP("scalar", lambda e, buf=buf, c=c: e.activation(
                sqscr[:, :], buf[:, :], AF.Square, accum_out=ltb[:, c + 1:c + 2]),
                waits=(la, lb2, consts_ready))
            s1 = vop(lambda e, c=c: e.tensor_single_scalar(
                ltb[:, c + 2:c + 3], ltb[:, c:c + 1], cstf[:, 0:1], ALU.mult),
                waits=(rs, sq))
            s2 = vop(lambda e, c=c: e.tensor_single_scalar(
                ltb[:, c + 3:c + 4], ltb[:, c + 1:c + 2], cstf[:, 0:1], ALU.mult),
                waits=(s1,))
            s3 = vop(lambda e, c=c: e.tensor_mul(
                ltb[:, c:c + 1], ltb[:, c + 2:c + 3], ltb[:, c + 2:c + 3]),
                waits=(s1,))
            s4 = vop(lambda e, c=c: e.tensor_sub(
                ltb[:, c + 1:c + 2], ltb[:, c + 3:c + 4], ltb[:, c:c + 1]),
                waits=(s2, s3))
            s5 = vop(lambda e, c=c: e.tensor_single_scalar(
                ltb[:, c + 3:c + 4], ltb[:, c + 1:c + 2], cstf[:, 1:2], ALU.add),
                waits=(s4,))
            s6 = vop(lambda e, c=c: e.tensor_copy(
                ltb[:, c + 1:c + 2], ltb[:, c + 3:c + 4]), waits=(s5,))
            s7a = OP("scalar", lambda e, c=c: e.activation(
                ltb[:, c:c + 1], ltb[:, c + 1:c + 2], AF.Ln), waits=(s6,))
            s7 = OP("scalar", lambda e, c=c: e.activation(
                ltb[:, c + 4:c + 5], ltb[:, c:c + 1], AF.Exp,
                scale=cstf[:, 15:16]), waits=(s7a,))
            s8 = vop(lambda e, c=c: e.tensor_mul(
                ltb[:, c + 1:c + 2], ltb[:, c + 2:c + 3], ltb[:, c + 4:c + 5]),
                waits=(s7, s1))
            s9 = vop(lambda e, c=c: e.tensor_single_scalar(
                ltb[:, c + 5:c + 6], ltb[:, c + 1:c + 2], cstf[:, 4:5], ALU.mult),
                waits=(s8,))
            ttw = (last_tt[rt - 2],) if rt >= 2 else ()
            xn = OP("scalar", lambda e, buf=buf, xnb=xnb, c=c: e.activation(
                xnb[:, :], buf[:, :], AF.Identity,
                bias=ltb[:, c + 5:c + 6], scale=ltb[:, c + 4:c + 5]),
                waits=(s9,) + ttw)
            ln_xn[rt] = xn
            tt = None
            for fb in range(4):
                sj = fb % 2
                for j in range(8):
                    ft = fb * 8 + j
                    tt = OP("tensor", lambda e, ft=ft, j=j, sj=sj, xnb=xnb: e.transpose(
                        SS[sj].ap().bitcast(BF16)[:, j * 128:(j + 1) * 128],
                        xnb[:, ft * 128:(ft + 1) * 128], ident[:, :]),
                        waits=(xn, cl) + bw("S%d" % sj))
                ev = vop(lambda e, fb=fb, rt=rt, sj=sj: e.tensor_copy(
                    AP(xnT.ap().tensor, fb * 8 * 512 + rt * 128,
                       [[32 * 512, 128], [512, 8], [1, 128]]),
                    AP(SS[sj].ap().bitcast(BF16).tensor, 0,
                       [[2048, 128], [128, 8], [1, 128]])),
                    waits=(tt,))
                bs("S%d" % sj, ev)
                xn_evac[(rt, fb)] = ev
            last_tt[rt] = tt
            if rt < 2:
                ln_load(rt + 2, "scalar", "dC", "scalar", "dC",
                        (ln_xn[rt], ln_rs[rt]))
        xnT_ready = tuple(xn_evac.values())
        ln_xt_done = (ln_xn[2], ln_rs[2], ln_xn[3], ln_rs[3])
        if "xnT" in taps and ph == 0:
            OP("sync", lambda e: e.dma_start(dtaps["xnT"], xnT[:, :]),
               waits=xnT_ready, sem="dS", dma=True)
        if "xnT1" in taps and ph == 1:
            OP("sync", lambda e: e.dma_start(dtaps["xnT1"], xnT[:, :]),
               waits=xnT_ready, sem="dS", dma=True)
        if "ltb1" in taps and ph == 1:
            OP("sync", lambda e: e.dma_start(dtaps["ltb1"], ltb[:, :]),
               waits=(ln_xn[0], ln_xn[1], ln_xn[2], ln_xn[3]), sem="dS", dma=True)

        if stage < 2:
            prev_phase_done = (last_xn,)
            continue
        ctx_ev = {}
        sweep_end = {}
        rope_tok = {}
        qp_tok = {}
        sm_last = {}
        dacc_guard = {}
        dacb_guard = [None]
        ring_guard = {}
        wq_cur = {0: wq_tok}

        def qproj_rope(gg, qh):
            mm = None
            for k in range(32):
                mm = OP("tensor", lambda e, k=k, qh=qh: e.matmul(
                    pq6[:, :],
                    wqslab[:, k * 512 + qh * 128: k * 512 + (qh + 1) * 128],
                    xnT[:, k * 512:(k + 1) * 512],
                    start=(k == 0), stop=(k == 31)),
                    waits=(wq_cur[gg],) + xnT_ready + bw("b6"))
            finish_qproj(gg, qh, mm)()

        def finish_qproj(gg, qh, mm):
            qp_tok[(gg, qh)] = mm
            qbc = OP("scalar", lambda e: e.copy(qb[:, :], pq6[:, :]), waits=(mm,))

            def rot_rest():
                rot = OP("tensor", lambda e: e.matmul(
                    pr7[:, :], mtr[:, :], qb[:, :], start=True, stop=True),
                    waits=(qbc, cl) + bw("b7"))
                v1 = vop(lambda e: e.tensor_mul(rt1[:, :], pq6[:, :], cos_ph[:, :]),
                         waits=(mm, sk, ck, qbc))
                bs("b6", qbc, v1)
                v2 = vop(lambda e: e.tensor_mul(rt2[:, :], pr7[:, :], sin_ph[:, :]),
                         waits=(rot,))
                bs("b7", v2)
                slotw = sm_last.get((gg - 1, qh))
                v3 = vop(lambda e, qh=qh: e.tensor_add(
                    qrope[:, qh * 512:(qh + 1) * 512], rt1[:, :], rt2[:, :]),
                    waits=(v1, v2) + ((slotw,) if slotw else ()))
                rope_tok[(gg, qh)] = v3
            return rot_rest

        filler = []
        tail_q = []
        tail_dq = []
        rotq = []

        def queue_qproj(gg, qh):
            st = {"mm": None}
            for k in range(32):
                def thunk(k=k, gg=gg, qh=qh, st=st):
                    st["mm"] = OP("tensor", lambda e, k=k, qh=qh: e.matmul(
                        pq6[:, :],
                        wqslab[:, k * 512 + qh * 128: k * 512 + (qh + 1) * 128],
                        xnT[:, k * 512:(k + 1) * 512],
                        start=(k == 0), stop=(k == 31)),
                        waits=(wq_cur[gg],) + xnT_ready + bw("b6"))
                    if k == 31:
                        rotq.append([1, finish_qproj(gg, qh, st["mm"])])
                filler.append(thunk)

        for qh in range(3):
            qproj_rope(0, qh)
        if ph == 0 and "qrope" in taps:
            OP("sync", lambda e: e.dma_start(dtaps["qrope"], qrope[:, 0:3 * 512]),
               waits=(rope_tok[(0, 2)],), sem="dS", dma=True)

        ngroups = 8 if stage >= 3 else 1
        for g in range(ngroups):
            b2 = g % 2
            ktr_last, vg_last = gather_done[g]
            if g == 0 and ph == 0:
                if "kt0" in taps:
                    OP("sync", lambda e: e.dma_start(dtaps["kt0"], KT[0][:, :]),
                       waits=(ktr_last,), sem="dS", dma=True)
                if "v0" in taps:
                    OP("sync", lambda e: e.dma_start(dtaps["v0"], VH[0][:, :]),
                       waits=(vg_last,), sem="dS", dma=True)

            for qh in range(4):
                if qh == 1 and g < ngroups - 1:
                    bfr = sweep_end.get(g - 1)
                    wq_cur[g + 1] = OP("gpsimd", lambda e, g=g: e.dma_start(
                        wqslab[:, :], dwq[g + 1]), waits=(qp_tok[(g, 3)],),
                        sem="dG", dma=True)
                    itk = make_idx(g + 1, ())
                    lnw = ln_xt_done if g == 0 else ()
                    plan_gathers(g + 1, (itk,) + ((bfr,) if bfr else ()) + lnw)
                tgt = g * 4 + qh + 3
                if tgt < ngroups * 4:
                    queue_qproj(tgt // 4, tgt % 4)
                ctxk = "b%d" % (qh % 2)
                ctxp = pb[qh % 2]
                dac = dacc[:, (qh % 2) * 512:(qh % 2 + 1) * 512]
                da = None
                ex_tok = {}
                cx_tok = {}
                defer_cx = []
                cx_l = None
                sm_l = None
                for c2 in range(34):
                  if c2 < 32:
                    j = c2 % 2
                    sk_ = "S%d" % j
                    sms = []
                    for hh in range(2):
                        c = 2 * c2 + hh
                        sm = OP("tensor", lambda e, c=c, qh=qh, j=j, hh=hh, b2=b2: e.matmul(
                            SS[j][:, hh * 512:(hh + 1) * 512],
                            KT[b2][:, c * 128:(c + 1) * 128],
                            qrope[:, qh * 512:(qh + 1) * 512],
                            start=True, stop=True),
                            waits=(rope_tok[(g, qh)], ktr_last) + bw(sk_))
                        sms.append(sm)
                    sm_l = sms[-1]
                    slot = c2 % 4
                    rw = ring_guard.get(slot, ())
                    ex = OP("scalar", lambda e, c2=c2, j=j, slot=slot: e.activation(
                        expr[:, slot * 1024:(slot + 1) * 1024], SS[j][:, :],
                        AF.Exp, bias=cstf[:, 2:3], scale=cstf[:, 3:4]),
                        waits=(sms[1], consts_ready) + tuple(rw))
                    ex_tok[c2] = ex
                    bs(sk_, ex)
                  if c2 >= 2:
                    p2 = c2 - 2
                    slotp = p2 % 4
                    exp_ = ex_tok[p2]
                    cxs = []
                    for hh in range(2):
                        c = 2 * p2 + hh
                        cxw = bw(ctxk) if c == 0 else ()
                        cx = OP("tensor", lambda e, c=c, hh=hh, slot=slotp, b2=b2,
                                ctxp=ctxp: e.matmul(
                            ctxp[:, :], VH[b2][:, c * 128:(c + 1) * 128],
                            expr[:, slot * 1024 + hh * 512: slot * 1024 + (hh + 1) * 512],
                            start=(c == 0), stop=(c == 63)),
                            waits=(exp_, vg_last) + cxw)
                        cxs.append(cx)
                    cx_l = cxs[-1]
                    cx_tok[p2] = cx_l
                    for (sslot, cc2) in defer_cx:
                        if cc2 == p2:
                            ring_guard[sslot].append(cx_l)
                    defer_cx = [d for d in defer_cx if d[1] != p2]
                  if c2 < 32 and c2 % 2 == 1:
                        sa = (c2 - 1) % 4
                        sb = c2 % 4
                        q1 = vop(lambda e, sa=sa: e.tensor_add(
                            datmp[:, 0:512], expr[:, sa * 1024:sa * 1024 + 512],
                            expr[:, sa * 1024 + 512:(sa + 1) * 1024]),
                            waits=(ex_tok[c2 - 1],))
                        q2 = vop(lambda e, sb=sb: e.tensor_add(
                            datmp[:, 512:1024], expr[:, sb * 1024:sb * 1024 + 512],
                            expr[:, sb * 1024 + 512:(sb + 1) * 1024]),
                            waits=(ex,))
                        q3 = vop(lambda e: e.tensor_add(
                            datmp[:, 1024:1536], datmp[:, 0:512],
                            datmp[:, 512:1024]), waits=(q1, q2))
                        if c2 == 1:
                            dgw = dacc_guard.get(qh % 2)
                            da = vop(lambda e, dac=dac: e.tensor_copy(
                                dac, datmp[:, 1024:1536]),
                                waits=(q3,) + ((dgw,) if dgw else ()))
                        else:
                            da = vop(lambda e, dac=dac: e.tensor_add(
                                dac, dac, datmp[:, 1024:1536]), waits=(q3, da))
                        ring_guard[sa] = [q1]
                        defer_cx.append((sa, c2 - 1))
                        ring_guard[sb] = [q2]
                        defer_cx.append((sb, c2))
                  if tail_dq:
                    tail_dq.pop(0)()
                  if rotq:
                    if rotq[0][0] <= 0:
                        rotq.pop(0)[1]()
                    else:
                        rotq[0][0] -= 1
                  if tail_q and c2 >= 8:
                    tail_q.pop(0)()
                  if filler and c2 >= 16:
                    filler.pop(0)()
                    if filler:
                        filler.pop(0)()
                sm_last[(g, qh)] = sm_l

                box = {}

                def denom_dct(box=box, qh=qh, dac=dac, da=da):
                    dbw = dacb_guard[0]
                    box["dct"] = vop(lambda e, dac=dac: e.tensor_copy(
                        dacb[:, :], dac), waits=(da,) + ((dbw,) if dbw else ()))
                    dacc_guard[qh % 2] = box["dct"]

                def denom_tail(box=box, g=g, qh=qh, ctxp=ctxp, cx_l=cx_l):
                    rd = OP("tensor", lambda e: e.matmul(
                        pr7[:, :], ones[:, :], dacb[:, :], start=True, stop=True),
                        waits=(box["dct"], cl) + bw("b7"))
                    dacb_guard[0] = rd
                    rca = OP("scalar", lambda e: e.activation(
                        lnscr[:, :], pr7[:, :], AF.Ln), waits=(rd,))
                    rc = OP("scalar", lambda e: e.activation(
                        recip[:, :], lnscr[:, :], AF.Exp, scale=cstf[:, 4:5]),
                        waits=(rca,))
                    bs("b7", rca)
                    cev = vop(lambda e, g=g, qh=qh, ctxp=ctxp: e.tensor_mul(
                        ctx[:, (g * 4 + qh) * 512:(g * 4 + qh + 1) * 512],
                        ctxp[:, :], recip[:, :]), waits=(rc, cx_l))
                    ctx_ev[(g, qh)] = cev
                    bs("b%d" % (qh % 2), cev)
                tail_dq.append(denom_dct)
                tail_q.append(denom_tail)
                sweep_end[g] = cx_l
                while rotq:
                    rotq.pop(0)[1]()
                if qh == 3 and g == ngroups - 1:
                    while tail_dq:
                        tail_dq.pop(0)()
                    while tail_q:
                        tail_q.pop(0)()
                    while filler:
                        filler.pop(0)()

        ctx_all = tuple(ctx_ev[(g, qh)] for g in range(ngroups) for qh in range(4))
        if stage < 3:
            prev_phase_done = ctx_all
            continue
        if "ctx" in taps and ph == 0:
            OP("sync", lambda e: e.dma_start(dtaps["ctx"], ctx[:, :]),
               waits=ctx_all, sem="dS", dma=True)

        wo_tok = [None, None]
        wo_tok[0] = OP("gpsimd", lambda e: e.dma_start(wo_sl[0][:, :], dwo[0]),
                       waits=(sweep_end[6],), sem="dG", dma=True)
        wo_tok[1] = OP("gpsimd", lambda e: e.dma_start(wo_sl[1][:, :], dwo[1]),
                       waits=(sweep_end[7],), sem="dG", dma=True)
        s_ev = {}
        womm = {}
        for og in range(8):
            if og >= 2:
                wo_tok[og % 2] = OP("gpsimd", lambda e, og=og: e.dma_start(
                    wo_sl[og % 2][:, :], dwo[og]),
                    waits=(womm[(og - 2, 3)],), sem="dG", dma=True)
            for m in range(4):
                sj = m % 2
                bk = "S%d" % sj
                mm = None
                for k in range(32):
                    mm = OP("tensor", lambda e, k=k, m=m, og=og, sj=sj: e.matmul(
                        SS[sj][:, 0:512],
                        wo_sl[og % 2][:, k * 512 + m * 128: k * 512 + (m + 1) * 128],
                        ctx[:, k * 512:(k + 1) * 512],
                        start=(k == 0), stop=(k == 31)),
                        waits=(wo_tok[og % 2],) + ctx_all + bw(bk))
                womm[(og, m)] = mm
                ev = OP("scalar", lambda e, og=og, m=m, sj=sj: e.copy(
                    s_h[:, (og * 4 + m) * 512:(og * 4 + m + 1) * 512],
                    SS[sj][:, 0:512]), waits=(mm,))
                s_ev[(og, m)] = ev
                bs(bk, ev)
        if "s" in taps and ph == 0:
            OP("sync", lambda e: e.dma_start(dtaps["s"], s_h[:, :]),
               waits=tuple(s_ev.values()), sem="dS", dma=True)
        s_last = s_ev[(7, 3)]

        gsl_tok = {}
        hev = {}
        gmm = {}
        for it in range(86):
            sl = it % 2
            gss = gslab[:, sl * 4096:(sl + 1) * 4096]
            uss = uslab[:, sl * 4096:(sl + 1) * 4096]
            if it < 2:
                ldw = (s_last, womm[(7, 3)])
            else:
                ldw = (gmm[it - 2],)
            gt = OP("gpsimd", lambda e, it=it, gss=gss: e.dma_start(gss, dgate[it]),
                    waits=ldw, sem="dG", dma=True)
            ut = OP("gpsimd", lambda e, it=it, uss=uss: e.dma_start(uss, dup[it]),
                    waits=ldw, sem="dG", dma=True)
            gsl_tok[it] = (gt, ut)
            gk_ = "b%d" % sl
            up = pq6 if sl == 0 else pr7
            uk = "b6" if sl == 0 else "b7"
            mmg = mmu = None
            for k in range(32):
                mmg = OP("tensor", lambda e, k=k, sl=sl, gss=gss: e.matmul(
                    pb[sl][:, :], gss[:, k * 128:(k + 1) * 128],
                    xnT[:, k * 512:(k + 1) * 512],
                    start=(k == 0), stop=(k == 31)),
                    waits=(gt,) + xnT_ready + bw(gk_))
            for k in range(32):
                mmu = OP("tensor", lambda e, k=k, up=up, uss=uss: e.matmul(
                    up[:, :], uss[:, k * 128:(k + 1) * 128],
                    xnT[:, k * 512:(k + 1) * 512],
                    start=(k == 0), stop=(k == 31)),
                    waits=(ut,) + bw(uk))
            gmm[it] = mmu
            uev = OP("scalar", lambda e, sl=sl, up=up: e.copy(
                utmp[:, sl * 512:(sl + 1) * 512], up[:, :]), waits=(mmu,))
            he = vop(lambda e, it=it, sl=sl: e.scalar_tensor_tensor(
                h_half[:, it * 512:(it + 1) * 512], pb[sl][:, :], gu[:, it:it + 1],
                utmp[:, sl * 512:(sl + 1) * 512], ALU.mult, ALU.mult),
                waits=(uev, mmg, consts_ready))
            hev[it] = he
            bs(gk_, he)
            bs(uk, uev, he)
        h_all = tuple(hev.values())
        if "h" in taps and ph == 0:
            OP("sync", lambda e: e.dma_start(
                dtaps["h"], h_half[:, 0:taps["h"][0][1]]),
               waits=h_all, sem="dS", dma=True)

        dsl = [dslabA, dslabB]
        dtok = [None, None]
        comb = {}
        dmm = {}
        fin_q = []
        out_box = [None]
        for mt in range(32):
            sl = mt % 2
            if mt < 2:
                ldw = (gmm[85], hev[85], gsl_tok[85][sl])
            else:
                ldw = (dmm[mt - 2],)
            dtok[sl] = OP("gpsimd", lambda e, mt=mt, sl=sl: e.dma_start(
                dsl[sl][:, :], ddown[mt]), waits=ldw, sem="dG", dma=True)
            xw = (comb[(mt - 2, 3)],) if mt >= 2 else \
                (womm[(7, 3)], gmm[85], gsl_tok[85][0], gsl_tok[85][1])
            xts = []
            for tt in range(4):
                xts.append(OP("sync", lambda e, mt=mt, tt=tt, sl=sl, ph=ph: e.dma_start(
                    xmt[:, sl * 512 + tt * 128: sl * 512 + (tt + 1) * 128],
                    dx[ph * 512 + tt * 128: ph * 512 + (tt + 1) * 128,
                       mt * 128:(mt + 1) * 128]),
                    waits=xw, sem="dS", dma=True))
            sj = sl
            bk = "S%d" % sj
            mm = None
            for ki in range(86):
                mm = OP("tensor", lambda e, ki=ki, sj=sj, sl=sl: e.matmul(
                    SS[sj][:, 0:512], dsl[sl][:, ki * 128:(ki + 1) * 128],
                    h_half[:, ki * 512:(ki + 1) * 512],
                    start=(ki == 0), stop=(ki == 85)),
                    waits=(dtok[sl],) + h_all + bw(bk))
                if ki == 12 and fin_q:
                    fin_q.pop(0)()
            dmm[mt] = mm
            si = OP("scalar", lambda e, mt=mt, sl=sl, sj=sj: e.activation(
                stmp[:, sl * 512:(sl + 1) * 512], SS[sj][:, 0:512], AF.Silu,
                scale=ds[:, mt:mt + 1]), waits=(mm, consts_ready))
            bs(bk, si)
            pe = vop(lambda e, mt=mt, sl=sl: e.tensor_mul(
                pbuf[:, sl * 512:(sl + 1) * 512],
                stmp[:, sl * 512:(sl + 1) * 512],
                s_h[:, mt * 512:(mt + 1) * 512]),
                waits=(si, s_ev[(mt // 4, mt % 4)]))
            pe_last[mt] = pe

            def fin(mt=mt, sl=sl, pe=pe, xts=xts, ph=ph):
                tbp = pq6 if sl == 0 else pr7
                tbk = "b6" if sl == 0 else "b7"
                trs = []
                for tt in range(4):
                    trs.append(OP("tensor", lambda e, tt=tt, sl=sl, tbp=tbp: e.transpose(
                        tbp.ap().bitcast(BF16)[:, tt * 128:(tt + 1) * 128],
                        pbuf[:, sl * 512 + tt * 128: sl * 512 + (tt + 1) * 128],
                        ident[:, :]),
                        waits=(pe, cl) + bw(tbk)))
                cas = []
                for tt in range(4):
                    cadd = vop(lambda e, tt=tt, sl=sl, tbp=tbp: e.tensor_add(
                        ostg[:, sl * 512 + tt * 128: sl * 512 + (tt + 1) * 128],
                        tbp.ap().bitcast(BF16)[:, tt * 128:(tt + 1) * 128],
                        xmt[:, sl * 512 + tt * 128: sl * 512 + (tt + 1) * 128]),
                        waits=(trs[3], xts[tt]))
                    comb[(mt, tt)] = cadd
                    cas.append(cadd)
                    out_box[0] = OP("sync", lambda e, mt=mt, tt=tt, sl=sl, ph=ph: e.dma_start(
                        dout[ph * 512 + tt * 128: ph * 512 + (tt + 1) * 128,
                             mt * 128:(mt + 1) * 128],
                        ostg[:, sl * 512 + tt * 128: sl * 512 + (tt + 1) * 128]),
                        waits=(cadd,), sem="dO", dma=True, force_inc=True)
                    out_tokens.append(out_box[0])
                bs(tbk, *cas)
            fin_q.append(fin)
        while fin_q:
            fin_q.pop(0)()

        prev_phase_done = (dmm[31], pe_last[30], pe_last[31],
                           comb[(31, 3)], comb[(30, 3)])

    if out_tokens:
        OP("sync", lambda e: e.dma_start(sqscr[:, 0:16], dx[0:128, 0:16]),
           waits=(out_tokens[-1],), sem="dS", dma=True)

    P.finalize()
    stack = contextlib.ExitStack()
    sems = {s: stack.enter_context(nc.semaphore(f"sem_{s}"))
            for s in sorted(P.sem_names)}
    with nc.Block() as blk:
        blk.sync(lambda e: P.emit("sync", e, sems))
        blk.gpsimd(lambda e: P.emit("gpsimd", e, sems))
        blk.tensor(lambda e: P.emit("tensor", e, sems))
        blk.vector(lambda e: P.emit("vector", e, sems))
        blk.scalar(lambda e: P.emit("scalar", e, sems))
    stack.close()
    return nc



def _byteslice_i8(w):
    w = np.ascontiguousarray(np.asarray(w).astype("<i4", copy=False))
    return w.view(np.int8).reshape(w.shape + (4,))[..., 0]


def _prep(inputs):
    hs = np.ascontiguousarray(np.asarray(inputs["L0_attn_hidden_states"], dtype=np.float32))
    Wq = np.ascontiguousarray(np.asarray(inputs["L0_attn_Wq"], dtype=np.float32))
    Wo = np.ascontiguousarray(np.asarray(inputs["L0_attn_Wo"], dtype=np.float32))
    kv = np.ascontiguousarray(np.asarray(inputs["L0_attn_kv_cache"], dtype=np.float32))
    bt = np.asarray(inputs["L0_attn_block_table"], dtype=np.int32)
    g8 = _byteslice_i8(inputs["L0_ffn_gate_wq"])
    u8 = _byteslice_i8(inputs["L0_ffn_up_wq"])
    d8 = _byteslice_i8(inputs["L0_ffn_down_wq"])
    gsc = np.asarray(inputs["L0_ffn_gate_scale"], dtype=np.float32)
    usc = np.asarray(inputs["L0_ffn_up_scale"], dtype=np.float32)
    dsc = np.asarray(inputs["L0_ffn_down_scale"], dtype=np.float32)

    wq_t = np.ascontiguousarray(Wq.reshape(32, 128, 8, 512).transpose(2, 1, 0, 3))
    wo_t = np.ascontiguousarray(Wo.reshape(32, 128, 8, 512).transpose(2, 1, 0, 3))
    gate_t = np.ascontiguousarray(
        np.ascontiguousarray(g8.T).reshape(32, 128, NIT, 128).transpose(2, 1, 0, 3))
    up_t = np.ascontiguousarray(
        np.ascontiguousarray(u8.T).reshape(32, 128, NIT, 128).transpose(2, 1, 0, 3))
    down_t = np.ascontiguousarray(
        np.ascontiguousarray(d8.T).reshape(NIT, 128, 32, 128).transpose(2, 1, 0, 3))
    gs_t = np.ascontiguousarray(gsc.reshape(NIT, 128).T)
    us_t = np.ascontiguousarray(usc.reshape(NIT, 128).T)
    ds_t = np.ascontiguousarray(dsc.reshape(32, 128).T)
    kvf = kv.reshape(ROWS_KV, 128)
    bt_rep = np.ascontiguousarray(bt.reshape(4, 128).T).astype(np.int32)
    jv = (np.arange(128) % 16).astype(np.float32).reshape(128, 1)
    ident = np.eye(128, dtype=BF)
    mtr = np.zeros((128, 128), dtype=BF)
    for a in range(64):
        mtr[64 + a, a] = -1.0
        mtr[a, 64 + a] = 1.0
    ones = np.ones((128, 128), dtype=BF)

    invf = 1.0 / (10000.0 ** (np.arange(0, D, 2, dtype=np.float64) / D))
    in_maps = []
    xs = hs.reshape(NCORE, NTOK, H)
    for c in range(NCORE):
        pos = (c % 2) * NTOK + np.arange(NTOK, dtype=np.float64)
        ang = pos[:, None] * invf[None, :]
        emb = np.concatenate([ang, ang], axis=1)
        cos_t = np.ascontiguousarray(np.cos(emb).T.astype(np.float32))
        sin_t = np.ascontiguousarray(np.sin(emb).T.astype(np.float32))
        in_maps.append(dict(
            x=xs[c], cos=cos_t, sin=sin_t, wq_t=wq_t, wo_t=wo_t,
            gate_t=gate_t, up_t=up_t, down_t=down_t, gs=gs_t, us=us_t, ds=ds_t,
            kv=kvf, bt_rep=bt_rep, jvec=jv, ident=ident, mt_rope=mtr, ones=ones))
    return in_maps


def kernel(**inputs):
    in_maps = _prep(inputs)
    nc = build_program()
    res = run_bass_kernel_spmd(nc, in_maps, core_ids=list(range(NCORE)))
    out = np.concatenate([np.asarray(r["out"]) for r in res.results], axis=0)
    return np.ascontiguousarray(out.reshape(B, S_, H).astype(np.float32))


# version 20 (final)
# speedup vs baseline: 1.0946x; 1.0494x over previous
import contextlib
import numpy as np
import ml_dtypes
import concourse.bass as bass
import concourse.mybir as mybir
from concourse.bass_utils import run_bass_kernel_spmd
from concourse.bass_core_rust import IndirectOffsetOnAxis
from concourse.ap import AP

F32 = mybir.dt.float32
BF16 = mybir.dt.bfloat16
I8 = mybir.dt.int8
I32 = mybir.dt.int32
AF = mybir.ActivationFunctionType
ALU = mybir.AluOpType
BF = ml_dtypes.bfloat16
AX = mybir.AxisListType.X

B, S_, H, NH, KVH, D, II = 4, 2048, 4096, 32, 8, 128, 11008
NB, BLK, BT = 1024, 16, 512
TKV = BT * BLK
NCORE = 8
NTOK = 1024
NIT = II // 128
ROWS_KV = NB * 2 * KVH * BLK

ENGS = ("sync", "gpsimd", "tensor", "vector", "scalar")


class _Op:
    __slots__ = ("fn", "eng", "sem", "inc_amt", "waits", "needed", "count")

    def __init__(self, fn, eng, sem, inc_amt, waits):
        self.fn, self.eng, self.sem, self.inc_amt = fn, eng, sem, inc_amt
        self.waits = [w for w in waits if w is not None]
        self.needed = False
        self.count = None


class Sched:

    def __init__(self):
        self.ops = {e: [] for e in ENGS}
        self.sem_names = set()

    def op(self, eng, fn, waits=(), sem=None, dma=False, force_inc=False):
        sem = sem or eng
        self.sem_names.add(sem)
        o = _Op(fn, eng, sem, 16 if dma else 1, waits)
        o.needed = force_inc or dma
        self.ops[eng].append(o)
        return o

    def finalize(self):
        for e in ENGS:
            for o in self.ops[e]:
                if e != "vector":
                    o.waits = [w for w in o.waits
                               if w.eng != e or w.inc_amt == 16]
                for w in o.waits:
                    w.needed = True
        counts = {s: 0 for s in self.sem_names}
        self.dma_sems = {}
        for e in ENGS:
            for o in self.ops[e]:
                if o.needed:
                    counts[o.sem] += o.inc_amt
                    o.count = counts[o.sem]
                if o.inc_amt == 16:
                    self.dma_sems[o.sem] = e
        self.final_counts = counts

    def emit(self, eng_name, eng, sems):
        satisfied = {}
        for o in self.ops[eng_name]:
            best = {}
            for w in o.waits:
                assert w.count is not None, "wait on op without inc"
                if w.count > best.get(w.sem, -1):
                    best[w.sem] = w.count
            for s in sorted(best):
                if best[s] > satisfied.get(s, -1):
                    eng.wait_ge(sems[s], best[s])
                    satisfied[s] = best[s]
            ins = o.fn(eng)
            if o.needed:
                ins.then_inc(sems[o.sem], o.inc_amt)
        for s, e in self.dma_sems.items():
            if e == eng_name and self.final_counts[s] > 0:
                eng.wait_ge(sems[s], self.final_counts[s])


def build_program(taps=None, stage=6, nphase=2):
    taps = taps or {}
    nc = bass.Bass("TRN2", target_bir_lowering=False, debug=False,
                   detect_race_conditions=False)

    dx = nc.dram_tensor("x", [NTOK, H], F32, kind="ExternalInput").ap()
    dcos = nc.dram_tensor("cos", [128, NTOK], F32, kind="ExternalInput").ap()
    dsin = nc.dram_tensor("sin", [128, NTOK], F32, kind="ExternalInput").ap()
    dwq = nc.dram_tensor("wq_t", [8, 128, 32, 512], F32, kind="ExternalInput").ap()
    dwo = nc.dram_tensor("wo_t", [8, 128, 32, 512], F32, kind="ExternalInput").ap()
    dgate = nc.dram_tensor("gate_t", [NIT, 128, 32, 128], I8, kind="ExternalInput").ap()
    dup = nc.dram_tensor("up_t", [NIT, 128, 32, 128], I8, kind="ExternalInput").ap()
    ddown = nc.dram_tensor("down_t", [32, 128, NIT, 128], I8, kind="ExternalInput").ap()
    dgs = nc.dram_tensor("gs", [128, NIT], F32, kind="ExternalInput").ap()
    dus = nc.dram_tensor("us", [128, NIT], F32, kind="ExternalInput").ap()
    dds = nc.dram_tensor("ds", [128, 32], F32, kind="ExternalInput").ap()
    dkv = nc.dram_tensor("kv", [ROWS_KV, 128], F32, kind="ExternalInput").ap()
    dbt = nc.dram_tensor("bt_rep", [128, 4], I32, kind="ExternalInput").ap()
    djv = nc.dram_tensor("jvec", [128, 1], F32, kind="ExternalInput").ap()
    dident = nc.dram_tensor("ident", [128, 128], BF16, kind="ExternalInput").ap()
    dmt = nc.dram_tensor("mt_rope", [128, 128], BF16, kind="ExternalInput").ap()
    dones = nc.dram_tensor("ones", [128, 128], BF16, kind="ExternalInput").ap()
    dout = nc.dram_tensor("out", [NTOK, H], F32, kind="ExternalOutput").ap()
    dtaps = {k: nc.dram_tensor("tap_" + k, shp, dt, kind="ExternalOutput").ap()
             for k, (shp, dt) in taps.items()}

    def sbt(name, offset, shape, dtype):
        return nc.alloc_sbuf_tensor_at(name, shape, dtype, offset=offset + 128)

    cos_ph = sbt("cos_ph", 16384, [128, 512], F32)
    sin_ph = sbt("sin_ph", 18432, [128, 512], F32)
    ident = sbt("ident", 20480, [128, 128], BF16)
    mtr = sbt("mtr", 20736, [128, 128], BF16)
    ones = sbt("ones", 20992, [128, 128], BF16)
    cstf = sbt("cstf", 21504, [128, 16], F32)
    gs = sbt("gs", 21632, [128, NIT], F32)
    us = sbt("us", 21984, [128, NIT], F32)
    gu = sbt("gu", 22336, [128, NIT], F32)
    ds = sbt("ds", 22688, [128, 32], F32)
    bt_rep = sbt("bt_rep", 22816, [128, 4], I32)
    jvec = sbt("jvec", 23072, [128, 1], F32)
    idxk2 = sbt("idxk2", 23360, [128, 8], I32)
    idxv2 = sbt("idxv2", 23872, [128, 8], I32)
    ltb = sbt("ltb", 24384, [128, 24], F32)
    dacc = sbt("dacc", 24480, [128, 2 * 512], F32)
    qrope = sbt("qrope", 28576, [128, 4 * 512], BF16)
    xnbufA = sbt("xnbufA", 24480, [128, 4096], BF16)
    qb = sbt("qb", 32672, [128, 512], BF16)
    rt1 = sbt("rt1", 33696, [128, 512], F32)
    rt2 = sbt("rt2", 35744, [128, 512], F32)
    recip = sbt("recip", 37792, [128, 512], F32)
    utmp = sbt("utmp", 24480, [128, 2 * 512], F32)
    stmp = sbt("stmp", 28576, [128, 2 * 512], F32)
    pbuf = sbt("pbuf", 32672, [128, 2 * 512], BF16)

    xnT = sbt("xnT", 39840, [128, 32 * 512], BF16)
    dslabA = sbt("dslabA", 39840, [128, NIT * 128], BF16)
    ctx = sbt("ctx", 72608, [128, 32 * 512], BF16)
    sqscr = sbt("sqscr", 72608, [128, 4096], F32)
    gslab = sbt("gslab", 72608, [128, 2 * 32 * 128], BF16)
    uslab = sbt("uslab", 88992, [128, 2 * 32 * 128], BF16)
    dslabB = sbt("dslabB", 72608, [128, NIT * 128], BF16)
    xmt = sbt("xmt", 94624, [128, 2 * 512], F32)
    ostg = sbt("ostg", 98720, [128, 2 * 512], F32)
    wqslab = sbt("wqslab", 105376, [128, 32 * 512], BF16)
    s_h = sbt("s_h", 105376, [128, 32 * 512], BF16)
    KT = [sbt("KT0", 138144, [128, TKV], BF16), sbt("KT1", 170912, [128, TKV], BF16)]
    VH = [sbt("V0", 154528, [128, TKV], BF16), sbt("V1", 187296, [128, TKV], BF16)]
    kst = sbt("kst", 203680, [128, 32 * 128], BF16)
    expr = sbt("expr", 211872, [128, 8 * 512], BF16)
    x_tA = sbt("x_tA", 170912, [128, 4096], F32)
    x_tB = sbt("x_tB", 187296, [128, 4096], F32)
    xnbufB = sbt("xnbufB", 211872, [128, 4096], BF16)
    wo_sl = [sbt("woA", 138144, [128, 32 * 512], BF16),
             sbt("woB", 170912, [128, 32 * 512], BF16)]
    h_half = sbt("h_half", 138144, [128, NIT * 512], BF16)
    idxf = sbt("idxf", 226208, [128, 4], F32)
    idxf2 = sbt("idxf2", 226464, [128, 4], F32)
    idxf3 = sbt("idxf3", 226720, [128, 4], F32)
    datmp = sbt("datmp", 220064, [128, 3 * 512], BF16)
    dacb = sbt("dacb", 223136, [128, 512], BF16)
    lnscr = sbt("lnscr", 224160, [128, 512], F32)

    pb = [nc.alloc_psum_tensor(f"pb{i}", [128, 512], F32) for i in range(2)]
    SS = [nc.alloc_psum_tensor(f"ps{i}", [128, 1024], F32) for i in range(2)]
    pq6 = nc.alloc_psum_tensor("pq6", [128, 512], F32)
    pr7 = nc.alloc_psum_tensor("pr7", [128, 512], F32)
    nc.sbuf_base = 227104

    P = Sched()
    bank_guard = {}

    def bw(k):
        return tuple(bank_guard.get(k, ()))

    def bs(k, *toks):
        bank_guard[k] = [t for t in toks if t is not None]

    def OP(eng, fn, waits=(), sem=None, dma=False, force_inc=False):
        return P.op(eng, fn, waits, sem=sem, dma=dma, force_inc=force_inc)

    def vop(fn, waits=()):
        return OP("vector", fn, waits)

    ld = []
    for dst, src in ((ident, dident), (mtr, dmt), (ones, dones), (gs, dgs),
                     (us, dus), (ds, dds), (bt_rep, dbt), (jvec, djv)):
        ld.append(OP("sync", lambda e, d=dst, s=src: e.dma_start(d[:, :], s),
                     sem="dS", dma=True))
    cl = ld[-1]
    CF = {"inv4096": 0, "eps": 1, "expbias": 2, "rsqrtd": 3, "neg1": 4, "negh": 15}
    m0 = None
    for k, v in (("inv4096", 1.0 / 4096), ("eps", 1e-5), ("expbias", -8.0),
                 ("rsqrtd", float(1.0 / np.sqrt(128.0))), ("neg1", -1.0),
                 ("negh", -0.5)):
        m0 = vop(lambda e, c=CF[k], v=v: e.memset(cstf[:, c:c + 1], v))
    for g in range(8):
        m0 = vop(lambda e, g=g: e.memset(cstf[:, 7 + g:8 + g], float(g * 16)))
    cvt = vop(lambda e: e.tensor_copy(idxf[:, :], bt_rep[:, :]), waits=(cl, m0))
    c256 = vop(lambda e: e.memset(cstf[:, 5:6], 256.0))
    c128 = vop(lambda e: e.memset(cstf[:, 6:7], 128.0))
    idxb = vop(lambda e: e.tensor_single_scalar(idxf2[:, :], idxf[:, :],
                                                cstf[:, 5:6], ALU.mult),
               waits=(cvt, c256, c128))
    gu_t = vop(lambda e: e.tensor_mul(gu[:, :], gs[:, :], us[:, :]), waits=(cl,))
    consts_ready = vop(lambda e: e.tensor_copy(sqscr[:, 0:1], cstf[:, 0:1]),
                       waits=(idxb, gu_t))

    idx_guard = {}

    def make_idx(g, waits):
        b = (g % 2) * 4
        iw = idx_guard.get(g % 2)
        t0 = vop(lambda e, g=g: e.tensor_single_scalar(
            idxf3[:, :], idxf2[:, :], cstf[:, 7 + g:8 + g], ALU.add),
            waits=tuple(waits) + (consts_ready, iw))
        t1 = vop(lambda e, b=b: e.tensor_copy(idxk2[:, b:b + 4], idxf3[:, :]),
                 waits=(t0,))
        tv = vop(lambda e: e.tensor_single_scalar(
            idxf[:, :], idxf3[:, :], cstf[:, 6:7], ALU.add), waits=(t0,))
        t2 = vop(lambda e, b=b: e.tensor_copy(idxv2[:, b:b + 4], idxf[:, :]),
                 waits=(tv, t1))
        return t2

    kst_guard = {}
    gather_done = {}

    def plan_gathers(g, waits, waits_v=None, mid=None):
        b2 = g % 2
        ib = b2 * 4
        last_tr = None
        last_v = None
        for q in range(4):
            slot = q % 2
            war = kst_guard.get(slot)
            gk = OP("gpsimd", lambda e, q=q, ib=ib, slot=slot: e.indirect_dma_start(
                kst[:, slot * 2048:(slot + 1) * 2048], None, dkv,
                IndirectOffsetOnAxis(idxk2[:, ib + q:ib + q + 1], 0)),
                waits=tuple(waits) + ((war,) if war else ()), sem="dG", dma=True)
            for j in range(16):
                c = q * 16 + j
                last_tr = OP("sync", lambda e, c=c, j=j, b2=b2, slot=slot:
                             e.dma_start_transpose(
                    KT[b2][:, c * 128:(c + 1) * 128],
                    kst[:, slot * 2048 + j * 128: slot * 2048 + (j + 1) * 128]),
                    waits=(gk,), sem="dT", dma=True)
            kst_guard[slot] = last_tr
        if mid is not None:
            mid()
        for q in range(4):
            last_v = OP("gpsimd", lambda e, q=q, b2=b2, ib=ib: e.indirect_dma_start(
                VH[b2][:, q * 2048:(q + 1) * 2048], None, dkv,
                IndirectOffsetOnAxis(idxv2[:, ib + q:ib + q + 1], 0)),
                waits=tuple(waits_v if waits_v is not None else waits),
                sem="dG", dma=True)
        idx_guard[b2] = last_v
        gather_done[g] = (last_tr, last_v)

    prev_phase_done = (consts_ready,)
    pe_last = {}
    out_tokens = []

    for ph in range(nphase):
        pw = tuple(prev_phase_done)
        lw = () if ph == 0 else pw

        xts_ln = {}
        xtile = [x_tA, x_tB]

        def ln_load(rt, q0, s0, q1, s1_, waits, ph=ph):
            buf = xtile[rt % 2]
            a = OP(q0, lambda e, rt=rt, buf=buf, ph=ph: e.dma_start(
                buf[:, 0:2048],
                dx[ph * 512 + rt * 128: ph * 512 + (rt + 1) * 128, 0:2048]),
                waits=waits, sem=s0, dma=True)
            b = OP(q1, lambda e, rt=rt, buf=buf, ph=ph: e.dma_start(
                buf[:, 2048:4096],
                dx[ph * 512 + rt * 128: ph * 512 + (rt + 1) * 128, 2048:4096]),
                waits=waits, sem=s1_, dma=True)
            xts_ln[rt] = (a, b)

        ln_load(0, "gpsimd", "dG", "scalar", "dC", lw)
        ln_load(1, "sync", "dS", "scalar", "dC", lw)
        ck = OP("sync", lambda e, ph=ph: e.dma_start(
            cos_ph[:, :], dcos[:, ph * 512:(ph + 1) * 512]), waits=pw,
            sem="dS", dma=True)
        sk = OP("sync", lambda e, ph=ph: e.dma_start(
            sin_ph[:, :], dsin[:, ph * 512:(ph + 1) * 512]), waits=pw,
            sem="dS", dma=True)

        it0 = make_idx(0, pw)
        wqbox = {}

        def _mid():
            wqbox["t"] = OP("gpsimd", lambda e: e.dma_start(
                wqslab[:, :], dwq[0]), waits=pw, sem="dG", dma=True)
        plan_gathers(0, pw + (it0,), mid=_mid)
        wq_tok = wqbox["t"]

        xn_evac = {}
        ln_xn = {}
        ln_rs = {}
        last_tt = {}
        for rt in range(4):
            buf = xtile[rt % 2]
            xnb = xnbufA if rt % 2 == 0 else xnbufB
            la, lb2 = xts_ln[rt]
            c = rt * 6
            rs = vop(lambda e, buf=buf, c=c: e.reduce_sum(
                ltb[:, c:c + 1], buf[:, :], AX), waits=(la, lb2))
            ln_rs[rt] = rs
            sq = OP("scalar", lambda e, buf=buf, c=c: e.activation(
                sqscr[:, :], buf[:, :], AF.Square, accum_out=ltb[:, c + 1:c + 2]),
                waits=(la, lb2, consts_ready))
            s1 = vop(lambda e, c=c: e.tensor_single_scalar(
                ltb[:, c + 2:c + 3], ltb[:, c:c + 1], cstf[:, 0:1], ALU.mult),
                waits=(rs, sq))
            s2 = vop(lambda e, c=c: e.tensor_single_scalar(
                ltb[:, c + 3:c + 4], ltb[:, c + 1:c + 2], cstf[:, 0:1], ALU.mult),
                waits=(s1,))
            s3 = vop(lambda e, c=c: e.tensor_mul(
                ltb[:, c:c + 1], ltb[:, c + 2:c + 3], ltb[:, c + 2:c + 3]),
                waits=(s1,))
            s4 = vop(lambda e, c=c: e.tensor_sub(
                ltb[:, c + 1:c + 2], ltb[:, c + 3:c + 4], ltb[:, c:c + 1]),
                waits=(s2, s3))
            s5 = vop(lambda e, c=c: e.tensor_single_scalar(
                ltb[:, c + 3:c + 4], ltb[:, c + 1:c + 2], cstf[:, 1:2], ALU.add),
                waits=(s4,))
            s6 = vop(lambda e, c=c: e.tensor_copy(
                ltb[:, c + 1:c + 2], ltb[:, c + 3:c + 4]), waits=(s5,))
            s7a = OP("scalar", lambda e, c=c: e.activation(
                ltb[:, c:c + 1], ltb[:, c + 1:c + 2], AF.Ln), waits=(s6,))
            s7 = OP("scalar", lambda e, c=c: e.activation(
                ltb[:, c + 4:c + 5], ltb[:, c:c + 1], AF.Exp,
                scale=cstf[:, 15:16]), waits=(s7a,))
            s8 = vop(lambda e, c=c: e.tensor_mul(
                ltb[:, c + 1:c + 2], ltb[:, c + 2:c + 3], ltb[:, c + 4:c + 5]),
                waits=(s7, s1))
            s9 = vop(lambda e, c=c: e.tensor_single_scalar(
                ltb[:, c + 5:c + 6], ltb[:, c + 1:c + 2], cstf[:, 4:5], ALU.mult),
                waits=(s8,))
            ttw = (last_tt[rt - 2],) if rt >= 2 else ()
            xnh0 = OP("scalar", lambda e, buf=buf, xnb=xnb, c=c: e.activation(
                xnb[:, 0:2048], buf[:, 0:2048], AF.Identity,
                bias=ltb[:, c + 5:c + 6], scale=ltb[:, c + 4:c + 5]),
                waits=(s9,) + ttw)
            xn = OP("scalar", lambda e, buf=buf, xnb=xnb, c=c: e.activation(
                xnb[:, 2048:4096], buf[:, 2048:4096], AF.Identity,
                bias=ltb[:, c + 5:c + 6], scale=ltb[:, c + 4:c + 5]),
                waits=(xnh0,))
            ln_xn[rt] = xn
            tt = None
            for fb in range(4):
                sj = fb % 2
                xw_ = xnh0 if fb < 2 else xn
                for j in range(8):
                    ft = fb * 8 + j
                    tt = OP("tensor", lambda e, ft=ft, j=j, sj=sj, xnb=xnb: e.transpose(
                        SS[sj].ap().bitcast(BF16)[:, j * 128:(j + 1) * 128],
                        xnb[:, ft * 128:(ft + 1) * 128], ident[:, :]),
                        waits=(xw_, cl) + bw("S%d" % sj))
                ev = vop(lambda e, fb=fb, rt=rt, sj=sj: e.tensor_copy(
                    AP(xnT.ap().tensor, fb * 8 * 512 + rt * 128,
                       [[32 * 512, 128], [512, 8], [1, 128]]),
                    AP(SS[sj].ap().bitcast(BF16).tensor, 0,
                       [[2048, 128], [128, 8], [1, 128]])),
                    waits=(tt,))
                bs("S%d" % sj, ev)
                xn_evac[(rt, fb)] = ev
            last_tt[rt] = tt
            if rt < 2:
                ln_load(rt + 2, "scalar", "dC", "scalar", "dC",
                        (ln_xn[rt], ln_rs[rt]))
        xnT_ready = tuple(xn_evac.values())
        ln_xt_done = (ln_xn[2], ln_rs[2], ln_xn[3], ln_rs[3])
        if "xnT" in taps and ph == 0:
            OP("sync", lambda e: e.dma_start(dtaps["xnT"], xnT[:, :]),
               waits=xnT_ready, sem="dS", dma=True)
        if "xnT1" in taps and ph == 1:
            OP("sync", lambda e: e.dma_start(dtaps["xnT1"], xnT[:, :]),
               waits=xnT_ready, sem="dS", dma=True)
        if "ltb1" in taps and ph == 1:
            OP("sync", lambda e: e.dma_start(dtaps["ltb1"], ltb[:, :]),
               waits=(ln_xn[0], ln_xn[1], ln_xn[2], ln_xn[3]), sem="dS", dma=True)

        if stage < 2:
            prev_phase_done = (last_xn,)
            continue
        ctx_ev = {}
        sweep_end = {}
        rope_tok = {}
        qp_tok = {}
        sm_last = {}
        dacc_guard = {}
        dacb_guard = [None]
        ring_guard = {}
        wq_cur = {0: wq_tok}

        def qproj_rope(gg, qh):
            mm = None
            for k in range(32):
                mm = OP("tensor", lambda e, k=k, qh=qh: e.matmul(
                    pq6[:, :],
                    wqslab[:, k * 512 + qh * 128: k * 512 + (qh + 1) * 128],
                    xnT[:, k * 512:(k + 1) * 512],
                    start=(k == 0), stop=(k == 31)),
                    waits=(wq_cur[gg],) + xnT_ready + bw("b6"))
            finish_qproj(gg, qh, mm)()

        def finish_qproj(gg, qh, mm):
            qp_tok[(gg, qh)] = mm
            qbc = OP("scalar", lambda e: e.copy(qb[:, :], pq6[:, :]), waits=(mm,))

            def rot_rest():
                rot = OP("tensor", lambda e: e.matmul(
                    pr7[:, :], mtr[:, :], qb[:, :], start=True, stop=True),
                    waits=(qbc, cl) + bw("b7"))
                v1 = vop(lambda e: e.tensor_mul(rt1[:, :], pq6[:, :], cos_ph[:, :]),
                         waits=(mm, sk, ck, qbc))
                bs("b6", qbc, v1)
                v2 = vop(lambda e: e.tensor_mul(rt2[:, :], pr7[:, :], sin_ph[:, :]),
                         waits=(rot,))
                bs("b7", v2)
                slotw = sm_last.get((gg - 1, qh))
                v3 = vop(lambda e, qh=qh: e.tensor_add(
                    qrope[:, qh * 512:(qh + 1) * 512], rt1[:, :], rt2[:, :]),
                    waits=(v1, v2) + ((slotw,) if slotw else ()))
                rope_tok[(gg, qh)] = v3
            return rot_rest

        filler = []
        tail_q = []
        tail_dq = []
        rotq = []

        def queue_qproj(gg, qh):
            st = {"mm": None}
            for k in range(32):
                def thunk(k=k, gg=gg, qh=qh, st=st):
                    st["mm"] = OP("tensor", lambda e, k=k, qh=qh: e.matmul(
                        pq6[:, :],
                        wqslab[:, k * 512 + qh * 128: k * 512 + (qh + 1) * 128],
                        xnT[:, k * 512:(k + 1) * 512],
                        start=(k == 0), stop=(k == 31)),
                        waits=(wq_cur[gg],) + xnT_ready + bw("b6"))
                    if k == 31:
                        rotq.append([1, finish_qproj(gg, qh, st["mm"])])
                filler.append(thunk)

        for qh in range(3):
            qproj_rope(0, qh)
        if ph == 0 and "qrope" in taps:
            OP("sync", lambda e: e.dma_start(dtaps["qrope"], qrope[:, 0:3 * 512]),
               waits=(rope_tok[(0, 2)],), sem="dS", dma=True)

        ngroups = 8 if stage >= 3 else 1
        for g in range(ngroups):
            b2 = g % 2
            ktr_last, vg_last = gather_done[g]
            if g == 0 and ph == 0:
                if "kt0" in taps:
                    OP("sync", lambda e: e.dma_start(dtaps["kt0"], KT[0][:, :]),
                       waits=(ktr_last,), sem="dS", dma=True)
                if "v0" in taps:
                    OP("sync", lambda e: e.dma_start(dtaps["v0"], VH[0][:, :]),
                       waits=(vg_last,), sem="dS", dma=True)

            for qh in range(4):
                if qh == 1 and g < ngroups - 1:
                    bfr = sweep_end.get(g - 1)
                    wq_cur[g + 1] = OP("gpsimd", lambda e, g=g: e.dma_start(
                        wqslab[:, :], dwq[g + 1]), waits=(qp_tok[(g, 3)],),
                        sem="dG", dma=True)
                    itk = make_idx(g + 1, ())
                    base = (itk,) + ((bfr,) if bfr else ())
                    if g == 0:
                        plan_gathers(g + 1, base + (ln_xn[2], ln_rs[2]),
                                     waits_v=base + (ln_xn[3], ln_rs[3]))
                    else:
                        plan_gathers(g + 1, base)
                tgt = g * 4 + qh + 3
                if tgt < ngroups * 4:
                    queue_qproj(tgt // 4, tgt % 4)
                ctxk = "b%d" % (qh % 2)
                ctxp = pb[qh % 2]
                dac = dacc[:, (qh % 2) * 512:(qh % 2 + 1) * 512]
                da = None
                ex_tok = {}
                cx_tok = {}
                defer_cx = []
                cx_l = None
                sm_l = None
                for c2 in range(34):
                  if c2 < 32:
                    j = c2 % 2
                    sk_ = "S%d" % j
                    sms = []
                    for hh in range(2):
                        c = 2 * c2 + hh
                        sm = OP("tensor", lambda e, c=c, qh=qh, j=j, hh=hh, b2=b2: e.matmul(
                            SS[j][:, hh * 512:(hh + 1) * 512],
                            KT[b2][:, c * 128:(c + 1) * 128],
                            qrope[:, qh * 512:(qh + 1) * 512],
                            start=True, stop=True),
                            waits=(rope_tok[(g, qh)], ktr_last) + bw(sk_))
                        sms.append(sm)
                    sm_l = sms[-1]
                    slot = c2 % 4
                    rw = ring_guard.get(slot, ())
                    ex = OP("scalar", lambda e, c2=c2, j=j, slot=slot: e.activation(
                        expr[:, slot * 1024:(slot + 1) * 1024], SS[j][:, :],
                        AF.Exp, bias=cstf[:, 2:3], scale=cstf[:, 3:4]),
                        waits=(sms[1], consts_ready) + tuple(rw))
                    ex_tok[c2] = ex
                    bs(sk_, ex)
                  if c2 >= 2:
                    p2 = c2 - 2
                    slotp = p2 % 4
                    exp_ = ex_tok[p2]
                    cxs = []
                    for hh in range(2):
                        c = 2 * p2 + hh
                        cxw = bw(ctxk) if c == 0 else ()
                        cx = OP("tensor", lambda e, c=c, hh=hh, slot=slotp, b2=b2,
                                ctxp=ctxp: e.matmul(
                            ctxp[:, :], VH[b2][:, c * 128:(c + 1) * 128],
                            expr[:, slot * 1024 + hh * 512: slot * 1024 + (hh + 1) * 512],
                            start=(c == 0), stop=(c == 63)),
                            waits=(exp_, vg_last) + cxw)
                        cxs.append(cx)
                    cx_l = cxs[-1]
                    cx_tok[p2] = cx_l
                    for (sslot, cc2) in defer_cx:
                        if cc2 == p2:
                            ring_guard[sslot].append(cx_l)
                    defer_cx = [d for d in defer_cx if d[1] != p2]
                  if c2 < 32 and c2 % 2 == 1:
                        sa = (c2 - 1) % 4
                        sb = c2 % 4
                        q1 = vop(lambda e, sa=sa: e.tensor_add(
                            datmp[:, 0:512], expr[:, sa * 1024:sa * 1024 + 512],
                            expr[:, sa * 1024 + 512:(sa + 1) * 1024]),
                            waits=(ex_tok[c2 - 1],))
                        q2 = vop(lambda e, sb=sb: e.tensor_add(
                            datmp[:, 512:1024], expr[:, sb * 1024:sb * 1024 + 512],
                            expr[:, sb * 1024 + 512:(sb + 1) * 1024]),
                            waits=(ex,))
                        q3 = vop(lambda e: e.tensor_add(
                            datmp[:, 1024:1536], datmp[:, 0:512],
                            datmp[:, 512:1024]), waits=(q1, q2))
                        if c2 == 1:
                            dgw = dacc_guard.get(qh % 2)
                            da = vop(lambda e, dac=dac: e.tensor_copy(
                                dac, datmp[:, 1024:1536]),
                                waits=(q3,) + ((dgw,) if dgw else ()))
                        else:
                            da = vop(lambda e, dac=dac: e.tensor_add(
                                dac, dac, datmp[:, 1024:1536]), waits=(q3, da))
                        ring_guard[sa] = [q1]
                        defer_cx.append((sa, c2 - 1))
                        ring_guard[sb] = [q2]
                        defer_cx.append((sb, c2))
                  if tail_dq:
                    tail_dq.pop(0)()
                  if rotq:
                    if rotq[0][0] <= 0:
                        rotq.pop(0)[1]()
                    else:
                        rotq[0][0] -= 1
                  if tail_q and c2 >= 8:
                    tail_q.pop(0)()
                  if filler and c2 >= 16:
                    filler.pop(0)()
                    if filler:
                        filler.pop(0)()
                sm_last[(g, qh)] = sm_l

                box = {}

                def denom_dct(box=box, qh=qh, dac=dac, da=da):
                    dbw = dacb_guard[0]
                    box["dct"] = vop(lambda e, dac=dac: e.tensor_copy(
                        dacb[:, :], dac), waits=(da,) + ((dbw,) if dbw else ()))
                    dacc_guard[qh % 2] = box["dct"]

                def denom_tail(box=box, g=g, qh=qh, ctxp=ctxp, cx_l=cx_l):
                    rd = OP("tensor", lambda e: e.matmul(
                        pr7[:, :], ones[:, :], dacb[:, :], start=True, stop=True),
                        waits=(box["dct"], cl) + bw("b7"))
                    dacb_guard[0] = rd
                    rca = OP("scalar", lambda e: e.activation(
                        lnscr[:, :], pr7[:, :], AF.Ln), waits=(rd,))
                    rc = OP("scalar", lambda e: e.activation(
                        recip[:, :], lnscr[:, :], AF.Exp, scale=cstf[:, 4:5]),
                        waits=(rca,))
                    bs("b7", rca)
                    cev = vop(lambda e, g=g, qh=qh, ctxp=ctxp: e.tensor_mul(
                        ctx[:, (g * 4 + qh) * 512:(g * 4 + qh + 1) * 512],
                        ctxp[:, :], recip[:, :]), waits=(rc, cx_l))
                    ctx_ev[(g, qh)] = cev
                    bs("b%d" % (qh % 2), cev)
                tail_dq.append(denom_dct)
                tail_q.append(denom_tail)
                sweep_end[g] = cx_l
                while rotq:
                    rotq.pop(0)[1]()
                if qh == 3 and g == ngroups - 1:
                    while tail_dq:
                        tail_dq.pop(0)()
                    while tail_q:
                        tail_q.pop(0)()
                    while filler:
                        filler.pop(0)()

        ctx_all = tuple(ctx_ev[(g, qh)] for g in range(ngroups) for qh in range(4))
        if stage < 3:
            prev_phase_done = ctx_all
            continue
        if "ctx" in taps and ph == 0:
            OP("sync", lambda e: e.dma_start(dtaps["ctx"], ctx[:, :]),
               waits=ctx_all, sem="dS", dma=True)

        wo_tok = [None, None]
        wo_tok[0] = OP("gpsimd", lambda e: e.dma_start(wo_sl[0][:, :], dwo[0]),
                       waits=(sweep_end[6],), sem="dG", dma=True)
        wo_tok[1] = OP("gpsimd", lambda e: e.dma_start(wo_sl[1][:, :], dwo[1]),
                       waits=(sweep_end[7],), sem="dG", dma=True)
        s_ev = {}
        womm = {}
        for og in range(8):
            if og >= 2:
                wo_tok[og % 2] = OP("gpsimd", lambda e, og=og: e.dma_start(
                    wo_sl[og % 2][:, :], dwo[og]),
                    waits=(womm[(og - 2, 3)],), sem="dG", dma=True)
            for m in range(4):
                sj = m % 2
                bk = "S%d" % sj
                mm = None
                for k in range(32):
                    mm = OP("tensor", lambda e, k=k, m=m, og=og, sj=sj: e.matmul(
                        SS[sj][:, 0:512],
                        wo_sl[og % 2][:, k * 512 + m * 128: k * 512 + (m + 1) * 128],
                        ctx[:, k * 512:(k + 1) * 512],
                        start=(k == 0), stop=(k == 31)),
                        waits=(wo_tok[og % 2],) + ctx_all + bw(bk))
                womm[(og, m)] = mm
                ev = OP("scalar", lambda e, og=og, m=m, sj=sj: e.copy(
                    s_h[:, (og * 4 + m) * 512:(og * 4 + m + 1) * 512],
                    SS[sj][:, 0:512]), waits=(mm,))
                s_ev[(og, m)] = ev
                bs(bk, ev)
        if "s" in taps and ph == 0:
            OP("sync", lambda e: e.dma_start(dtaps["s"], s_h[:, :]),
               waits=tuple(s_ev.values()), sem="dS", dma=True)
        s_last = s_ev[(7, 3)]

        gsl_tok = {}
        hev = {}
        gmm = {}
        for it in range(86):
            sl = it % 2
            gss = gslab[:, sl * 4096:(sl + 1) * 4096]
            uss = uslab[:, sl * 4096:(sl + 1) * 4096]
            if it < 2:
                ldw = (s_last, womm[(7, 3)])
            else:
                ldw = (gmm[it - 2],)
            gt = OP("gpsimd", lambda e, it=it, gss=gss: e.dma_start(gss, dgate[it]),
                    waits=ldw, sem="dG", dma=True)
            ut = OP("gpsimd", lambda e, it=it, uss=uss: e.dma_start(uss, dup[it]),
                    waits=ldw, sem="dG", dma=True)
            gsl_tok[it] = (gt, ut)
            gk_ = "b%d" % sl
            up = pq6 if sl == 0 else pr7
            uk = "b6" if sl == 0 else "b7"
            mmg = mmu = None
            for k in range(32):
                mmg = OP("tensor", lambda e, k=k, sl=sl, gss=gss: e.matmul(
                    pb[sl][:, :], gss[:, k * 128:(k + 1) * 128],
                    xnT[:, k * 512:(k + 1) * 512],
                    start=(k == 0), stop=(k == 31)),
                    waits=(gt,) + xnT_ready + bw(gk_))
            for k in range(32):
                mmu = OP("tensor", lambda e, k=k, up=up, uss=uss: e.matmul(
                    up[:, :], uss[:, k * 128:(k + 1) * 128],
                    xnT[:, k * 512:(k + 1) * 512],
                    start=(k == 0), stop=(k == 31)),
                    waits=(ut,) + bw(uk))
            gmm[it] = mmu
            uev = OP("scalar", lambda e, sl=sl, up=up: e.copy(
                utmp[:, sl * 512:(sl + 1) * 512], up[:, :]), waits=(mmu,))
            he = vop(lambda e, it=it, sl=sl: e.scalar_tensor_tensor(
                h_half[:, it * 512:(it + 1) * 512], pb[sl][:, :], gu[:, it:it + 1],
                utmp[:, sl * 512:(sl + 1) * 512], ALU.mult, ALU.mult),
                waits=(uev, mmg, consts_ready))
            hev[it] = he
            bs(gk_, he)
            bs(uk, uev, he)
        h_all = tuple(hev.values())
        if "h" in taps and ph == 0:
            OP("sync", lambda e: e.dma_start(
                dtaps["h"], h_half[:, 0:taps["h"][0][1]]),
               waits=h_all, sem="dS", dma=True)

        dsl = [dslabA, dslabB]
        dtok = [None, None]
        comb = {}
        dmm = {}
        fin_q = []
        out_box = [None]
        for mt in range(32):
            sl = mt % 2
            if mt < 2:
                ldw = (gmm[85], hev[85], gsl_tok[85][sl])
            else:
                ldw = (dmm[mt - 2],)
            dtok[sl] = OP("gpsimd", lambda e, mt=mt, sl=sl: e.dma_start(
                dsl[sl][:, :], ddown[mt]), waits=ldw, sem="dG", dma=True)
            xw = (comb[(mt - 2, 3)],) if mt >= 2 else \
                (womm[(7, 3)], gmm[85], gsl_tok[85][0], gsl_tok[85][1])
            xts = []
            for tt in range(4):
                xts.append(OP("sync", lambda e, mt=mt, tt=tt, sl=sl, ph=ph: e.dma_start(
                    xmt[:, sl * 512 + tt * 128: sl * 512 + (tt + 1) * 128],
                    dx[ph * 512 + tt * 128: ph * 512 + (tt + 1) * 128,
                       mt * 128:(mt + 1) * 128]),
                    waits=xw, sem="dS", dma=True))
            sj = sl
            bk = "S%d" % sj
            mm = None
            for ki in range(86):
                mm = OP("tensor", lambda e, ki=ki, sj=sj, sl=sl: e.matmul(
                    SS[sj][:, 0:512], dsl[sl][:, ki * 128:(ki + 1) * 128],
                    h_half[:, ki * 512:(ki + 1) * 512],
                    start=(ki == 0), stop=(ki == 85)),
                    waits=(dtok[sl],) + h_all + bw(bk))
                if ki == 12 and fin_q:
                    fin_q.pop(0)()
            dmm[mt] = mm
            si = OP("scalar", lambda e, mt=mt, sl=sl, sj=sj: e.activation(
                stmp[:, sl * 512:(sl + 1) * 512], SS[sj][:, 0:512], AF.Silu,
                scale=ds[:, mt:mt + 1]), waits=(mm, consts_ready))
            bs(bk, si)
            pe = vop(lambda e, mt=mt, sl=sl: e.tensor_mul(
                pbuf[:, sl * 512:(sl + 1) * 512],
                stmp[:, sl * 512:(sl + 1) * 512],
                s_h[:, mt * 512:(mt + 1) * 512]),
                waits=(si, s_ev[(mt // 4, mt % 4)]))
            pe_last[mt] = pe

            def fin(mt=mt, sl=sl, pe=pe, xts=xts, ph=ph):
                tbp = pq6 if sl == 0 else pr7
                tbk = "b6" if sl == 0 else "b7"
                trs = []
                for tt in range(4):
                    trs.append(OP("tensor", lambda e, tt=tt, sl=sl, tbp=tbp: e.transpose(
                        tbp.ap().bitcast(BF16)[:, tt * 128:(tt + 1) * 128],
                        pbuf[:, sl * 512 + tt * 128: sl * 512 + (tt + 1) * 128],
                        ident[:, :]),
                        waits=(pe, cl) + bw(tbk)))
                cas = []
                for tt in range(4):
                    cadd = vop(lambda e, tt=tt, sl=sl, tbp=tbp: e.tensor_add(
                        ostg[:, sl * 512 + tt * 128: sl * 512 + (tt + 1) * 128],
                        tbp.ap().bitcast(BF16)[:, tt * 128:(tt + 1) * 128],
                        xmt[:, sl * 512 + tt * 128: sl * 512 + (tt + 1) * 128]),
                        waits=(trs[3], xts[tt]))
                    comb[(mt, tt)] = cadd
                    cas.append(cadd)
                    out_box[0] = OP("sync", lambda e, mt=mt, tt=tt, sl=sl, ph=ph: e.dma_start(
                        dout[ph * 512 + tt * 128: ph * 512 + (tt + 1) * 128,
                             mt * 128:(mt + 1) * 128],
                        ostg[:, sl * 512 + tt * 128: sl * 512 + (tt + 1) * 128]),
                        waits=(cadd,), sem="dO", dma=True, force_inc=True)
                    out_tokens.append(out_box[0])
                bs(tbk, *cas)
            fin_q.append(fin)
        while fin_q:
            fin_q.pop(0)()

        prev_phase_done = (dmm[31], pe_last[30], pe_last[31],
                           comb[(31, 3)], comb[(30, 3)])

    if out_tokens:
        OP("sync", lambda e: e.dma_start(sqscr[:, 0:16], dx[0:128, 0:16]),
           waits=(out_tokens[-1],), sem="dS", dma=True)

    P.finalize()
    stack = contextlib.ExitStack()
    sems = {s: stack.enter_context(nc.semaphore(f"sem_{s}"))
            for s in sorted(P.sem_names)}
    with nc.Block() as blk:
        blk.sync(lambda e: P.emit("sync", e, sems))
        blk.gpsimd(lambda e: P.emit("gpsimd", e, sems))
        blk.tensor(lambda e: P.emit("tensor", e, sems))
        blk.vector(lambda e: P.emit("vector", e, sems))
        blk.scalar(lambda e: P.emit("scalar", e, sems))
    stack.close()
    return nc



def _byteslice_i8(w):
    w = np.ascontiguousarray(np.asarray(w).astype("<i4", copy=False))
    return w.view(np.int8).reshape(w.shape + (4,))[..., 0]


def _prep(inputs):
    hs = np.ascontiguousarray(np.asarray(inputs["L0_attn_hidden_states"], dtype=np.float32))
    Wq = np.ascontiguousarray(np.asarray(inputs["L0_attn_Wq"], dtype=np.float32))
    Wo = np.ascontiguousarray(np.asarray(inputs["L0_attn_Wo"], dtype=np.float32))
    kv = np.ascontiguousarray(np.asarray(inputs["L0_attn_kv_cache"], dtype=np.float32))
    bt = np.asarray(inputs["L0_attn_block_table"], dtype=np.int32)
    g8 = _byteslice_i8(inputs["L0_ffn_gate_wq"])
    u8 = _byteslice_i8(inputs["L0_ffn_up_wq"])
    d8 = _byteslice_i8(inputs["L0_ffn_down_wq"])
    gsc = np.asarray(inputs["L0_ffn_gate_scale"], dtype=np.float32)
    usc = np.asarray(inputs["L0_ffn_up_scale"], dtype=np.float32)
    dsc = np.asarray(inputs["L0_ffn_down_scale"], dtype=np.float32)

    wq_t = np.ascontiguousarray(Wq.reshape(32, 128, 8, 512).transpose(2, 1, 0, 3))
    wo_t = np.ascontiguousarray(Wo.reshape(32, 128, 8, 512).transpose(2, 1, 0, 3))
    gate_t = np.ascontiguousarray(
        np.ascontiguousarray(g8.T).reshape(32, 128, NIT, 128).transpose(2, 1, 0, 3))
    up_t = np.ascontiguousarray(
        np.ascontiguousarray(u8.T).reshape(32, 128, NIT, 128).transpose(2, 1, 0, 3))
    down_t = np.ascontiguousarray(
        np.ascontiguousarray(d8.T).reshape(NIT, 128, 32, 128).transpose(2, 1, 0, 3))
    gs_t = np.ascontiguousarray(gsc.reshape(NIT, 128).T)
    us_t = np.ascontiguousarray(usc.reshape(NIT, 128).T)
    ds_t = np.ascontiguousarray(dsc.reshape(32, 128).T)
    kvf = kv.reshape(ROWS_KV, 128)
    bt_rep = np.ascontiguousarray(bt.reshape(4, 128).T).astype(np.int32)
    jv = (np.arange(128) % 16).astype(np.float32).reshape(128, 1)
    ident = np.eye(128, dtype=BF)
    mtr = np.zeros((128, 128), dtype=BF)
    for a in range(64):
        mtr[64 + a, a] = -1.0
        mtr[a, 64 + a] = 1.0
    ones = np.ones((128, 128), dtype=BF)

    invf = 1.0 / (10000.0 ** (np.arange(0, D, 2, dtype=np.float64) / D))
    in_maps = []
    xs = hs.reshape(NCORE, NTOK, H)
    for c in range(NCORE):
        pos = (c % 2) * NTOK + np.arange(NTOK, dtype=np.float64)
        ang = pos[:, None] * invf[None, :]
        emb = np.concatenate([ang, ang], axis=1)
        cos_t = np.ascontiguousarray(np.cos(emb).T.astype(np.float32))
        sin_t = np.ascontiguousarray(np.sin(emb).T.astype(np.float32))
        in_maps.append(dict(
            x=xs[c], cos=cos_t, sin=sin_t, wq_t=wq_t, wo_t=wo_t,
            gate_t=gate_t, up_t=up_t, down_t=down_t, gs=gs_t, us=us_t, ds=ds_t,
            kv=kvf, bt_rep=bt_rep, jvec=jv, ident=ident, mt_rope=mtr, ones=ones))
    return in_maps


def kernel(**inputs):
    in_maps = _prep(inputs)
    nc = build_program()
    res = run_bass_kernel_spmd(nc, in_maps, core_ids=list(range(NCORE)))
    out = np.concatenate([np.asarray(r["out"]) for r in res.results], axis=0)
    return np.ascontiguousarray(out.reshape(B, S_, H).astype(np.float32))
